# version 12
# baseline (speedup 1.0000x reference)
import contextlib

import numpy as np
import ml_dtypes

import concourse.bass as bass
import concourse.mybir as mybir
from concourse.bass_utils import run_bass_kernel_spmd

F32 = mybir.dt.float32
BF16 = mybir.dt.bfloat16
F8E5 = mybir.dt.float8e5
AF = mybir.ActivationFunctionType
ALU = mybir.AluOpType

P = 128
D = 1024
T = 2048
NB = 2
NST = 16
HQ = 1024
TH = 256
S2 = 512
DF = 4096
NCORE = 8
EPS = 1e-5


class Ctr:
    def __init__(self, sem, step):
        self.sem = sem
        self.n = 0
        self.step = step

    def inc(self, instr):
        instr.then_inc(self.sem, self.step)
        self.n += self.step
        return self.n


def build_program():
    nc = bass.Bass("TRN2", target_bir_lowering=False, debug=False)
    dt = nc.dram_tensor

    x_fm = dt("x_fm", [NB, D, T], F32, kind="ExternalInput").ap()
    w_in = dt("w_in", [D, 2 * D], F32, kind="ExternalInput").ap()
    w_out = dt("w_out", [D, D], F32, kind="ExternalInput").ap()
    wxp = dt("wxp", [D, 65], F32, kind="ExternalInput").ap()
    w1t = dt("w1t", [D, DF], F32, kind="ExternalInput").ap()
    w2t = dt("w2t", [DF, D], F32, kind="ExternalInput").ap()
    loga_r = dt("loga_r", [P, 128], F32, kind="ExternalInput").ap()
    dgconv = dt("dgconv", [8, 4, P, P], F32, kind="ExternalInput").ap()
    dgd = dt("dgd", [8, P, P], F32, kind="ExternalInput").ap()
    cbcol = dt("cbcol", [P, 8], F32, kind="ExternalInput").ap()
    b1r = dt("b1r", [P, 32], F32, kind="ExternalInput").ap()
    b2r = dt("b2r", [P, 8], F32, kind="ExternalInput").ap()
    mask_fm = dt("mask_fm", [NB, D, T], BF16, kind="ExternalInput").ap()
    i128 = dt("i128", [P, P], BF16, kind="ExternalInput").ap()
    onesln = dt("onesln", [P, 1], BF16, kind="ExternalInput").ap()

    out_fm = dt("out_fm", [NB, D, T], F32, kind="ExternalOutput").ap()
    xhat_scr = dt("xhat_scr", [NB, D, T], F32, kind="Internal").ap()
    scr_b = dt("scr_b", [NB, 2, NST, HQ], BF16, kind="Internal").ap()
    scr_c = dt("scr_c", [NB, 2, NST, HQ], BF16, kind="Internal").ap()
    scr_d = dt("scr_d", [NB, 2, 1, HQ], F32, kind="Internal").ap()
    row_scr = dt("row_scr", [1, 2 * TH], F32, kind="Internal").ap()
    row_scr2 = dt("row_scr2", [1, 2 * S2], F32, kind="Internal").ap()

    st = contextlib.ExitStack()
    with st:
        e = st.enter_context
        dS = Ctr(e(nc.semaphore("dS")), 16)
        dG = Ctr(e(nc.semaphore("dG")), 16)
        sE = Ctr(e(nc.semaphore("sE")), 1)
        vE = Ctr(e(nc.semaphore("vE")), 1)
        tE = Ctr(e(nc.semaphore("tE")), 1)
        dX = Ctr(e(nc.semaphore("dX")), 16)
        dW = [Ctr(e(nc.semaphore(f"dW{i}")), 16) for i in range(2)]
        dO = [Ctr(e(nc.semaphore(f"dO{i}")), 16) for i in range(2)]
        dB = [Ctr(e(nc.semaphore(f"dB{i}")), 16) for i in range(2)]
        dC = [Ctr(e(nc.semaphore(f"dC{i}")), 16) for i in range(2)]
        dR = Ctr(e(nc.semaphore("dR")), 16)
        dRW = Ctr(e(nc.semaphore("dRW")), 16)
        dD = Ctr(e(nc.semaphore("dD")), 16)
        dSC = Ctr(e(nc.semaphore("dSC")), 16)
        dH = Ctr(e(nc.semaphore("dH")), 16)
        block = e(nc.Block())

        def barrier():
            ctrs = [dS, dG, sE, vE, tE, dX, dR, dRW, dD, dSC, dH] + dW + dO + dB + dC
            waits = [(c.sem, c.n) for c in ctrs]

            def emit(eng):
                for sem, n in waits:
                    eng.wait_ge(sem, n)
            block.sync(emit)
            block.gpsimd(emit)
            block.scalar(emit)
            block.vector(emit)
            block.tensor(emit)

        a_sb = e(nc.sbuf_tensor([P, 128], F32))
        b1q_sb = e(nc.sbuf_tensor([P, 32], F32))
        b2q_sb = e(nc.sbuf_tensor([P, 8], F32))
        b1f8 = e(nc.sbuf_tensor([P, 32], F8E5))
        b2f8 = e(nc.sbuf_tensor([P, 8], F8E5))
        i128_sb = e(nc.sbuf_tensor([P, P], BF16))
        ones_sb = e(nc.sbuf_tensor([P, 1], BF16))
        eps_sb = e(nc.sbuf_tensor([P, 1], F32))
        negh_sb = e(nc.sbuf_tensor([P, 1], F32))
        pp = [e(nc.psum_tensor(f"pp{i}", [P, 512], F32)) for i in range(8)]

        with contextlib.ExitStack() as s1:
            e1 = s1.enter_context
            dg_sb = e1(nc.sbuf_tensor([P, 8, 4, P], BF16))
            dd_sb = e1(nc.sbuf_tensor([P, 8, P], BF16))
            wxp_sb = e1(nc.sbuf_tensor([P, 8, 65], BF16))
            cb_sb = e1(nc.sbuf_tensor([P, 8], F32))
            loga_sb = e1(nc.sbuf_tensor([P, 128], F32))

            x_bf = e1(nc.sbuf_tensor([P, 8, HQ], BF16))
            ubuf = e1(nc.sbuf_tensor([P, 8 * HQ], BF16))
            xm_t = e1(nc.sbuf_tensor([P, 8, HQ + 3], BF16))
            halo = e1(nc.sbuf_tensor([P, 8, 3], BF16))
            x_conv = e1(nc.sbuf_tensor([P, 8, HQ], BF16))
            silu_z = e1(nc.sbuf_tensor([P, 8, HQ], BF16))
            dlb = e1(nc.sbuf_tensor([P, HQ], F32))
            b_bf = e1(nc.sbuf_tensor([NST, HQ], BF16))
            bd16 = e1(nc.sbuf_tensor([NST, HQ], F32))
            bdel_bf = e1(nc.sbuf_tensor([NST, HQ], BF16))
            c_bf = e1(nc.sbuf_tensor([NST, HQ], BF16))
            erow = e1(nc.sbuf_tensor([1, HQ], F32))
            uf32 = e1(nc.sbuf_tensor([P, 4096], F32))
            db_b = [e1(nc.sbuf_tensor(f"db{i}", [P, HQ], BF16)) for i in range(2)]
            h_b = [e1(nc.sbuf_tensor(f"hb{i}", [P, HQ], BF16)) for i in range(2)]
            z_b = [e1(nc.sbuf_tensor(f"zb{i}", [P, HQ], BF16)) for i in range(4)]
            hcar = e1(nc.sbuf_tensor([P, 128], F32))
            t3 = [e1(nc.sbuf_tensor(f"t3{i}", [P, 8, HQ], BF16)) for i in range(2)]
            win_b = [e1(nc.sbuf_tensor(f"win{i}", [P, 8, 128], BF16)) for i in range(2)]
            wo_b = [e1(nc.sbuf_tensor(f"wo{i}", [P, 8, 128], BF16)) for i in range(2)]
            xres = e1(nc.sbuf_tensor([P, 8, TH], F32))
            xho = e1(nc.sbuf_tensor([P, 8, TH], F32))
            rsbf = e1(nc.sbuf_tensor([P, 2, 2 * TH], BF16))
            mrow = e1(nc.sbuf_tensor([1, TH], F32))
            srow = e1(nc.sbuf_tensor([1, TH], F32))
            mrb = e1(nc.sbuf_tensor([P, TH], F32))
            rsb = e1(nc.sbuf_tensor([P, TH], F32))

            xm = xm_t[:]
            bbv = [ubuf[:, s * 2 * HQ:(s + 1) * 2 * HQ]
                   .rearrange("p (n t) -> p n t", n=2) for s in range(2)]
            cbv = [ubuf[:, (4 + 2 * s) * HQ:(6 + 2 * s) * HQ]
                   .rearrange("p (n t) -> p n t", n=2) for s in range(2)]
            res = uf32[:, 2048:4096].rearrange("p (g t) -> p g t", g=8)
            da_v = [uf32[:, 0:HQ], uf32[:, HQ:2 * HQ]]

            def g_(g):
                dG.inc(g.dma_start(out=dg_sb[:],
                                   in_=dgconv.rearrange("g j k p -> k g j p")))
                dG.inc(g.dma_start(out=dd_sb[:],
                                   in_=dgd.rearrange("g k p -> k g p")))
                dG.inc(g.dma_start(out=wxp_sb[:],
                                   in_=wxp.rearrange("(g p) o -> p g o", p=P)))
                dG.inc(g.dma_start(out=b1f8[:], in_=b1r))
                dG.inc(g.dma_start(out=b2f8[:], in_=b2r))
            block.gpsimd(g_)

            def y0_(y):
                dS.inc(y.dma_start(out=loga_sb[:], in_=loga_r))
                dS.inc(y.dma_start(out=cb_sb[:], in_=cbcol))
                dS.inc(y.dma_start(out=i128_sb[:], in_=i128))
                dS.inc(y.dma_start(out=ones_sb[:], in_=onesln))
            block.sync(y0_)

            def s0_(s):
                s.wait_ge(dS.sem, dS.n)
                s.wait_ge(dG.sem, dG.n)
                s.activation(out=a_sb[:], in_=loga_sb[:], func=AF.Exp)
                s.copy(out=b1q_sb[:], in_=b1f8[:])
                ii = s.copy(out=b2q_sb[:], in_=b2f8[:])
                sE.inc(ii)
            block.scalar(s0_)

            def v0_(v):
                v.wait_ge(sE.sem, sE.n)
                v.memset(eps_sb[:], EPS)
                v.memset(negh_sb[:], -0.5)
                ii = v.tensor_scalar_mul(out=a_sb[:], in0=a_sb[:], scalar1=-1.0)
                vE.inc(ii)
            block.vector(v0_)

            PLAN = []
            C = {}
            CTRS = {"S": sE, "V": vE, "T": tE, "Y": dS, "G": dG,
                    "X": dX, "W0": dW[0], "W1": dW[1], "O0": dO[0], "O1": dO[1],
                    "B0": dB[0], "B1": dB[1], "C0": dC[0], "C1": dC[1],
                    "R": dR, "RW": dRW, "DD": dD, "SC": dSC, "H": dH}

            def op(eng, incs, emitter):
                PLAN.append((eng, incs, emitter))

            def W(e, label):
                if label in C:
                    cname, cnt = C[label]
                    e.wait_ge(CTRS[cname].sem, cnt)
                else:
                    assert not (0 <= label[1] < 4), ("missing label", label)

            steps = [(b, q) for b in range(NB) for q in range(2)]

            def plan_front(S, prev):
                b, q = steps[S]
                tok0 = q * HQ

                def mk_xld(half):
                    def em(e, S=S, prev=prev, b=b, tok0=tok0, half=half):
                        W(e, ("xzmm", prev, 31))
                        dX.inc(e.dma_start(
                            out=x_bf[:, half * 4:(half + 1) * 4, :],
                            in_=x_fm[b].rearrange("(g p) t -> p g t", p=P)[
                                :, half * 4:(half + 1) * 4, tok0:tok0 + HQ]))
                    return em

                def em_halo(e, S=S, prev=prev, q=q):
                    W(e, ("convmm", prev, 15))
                    if q == 0:
                        ii = e.memset(xm[:, :, 0:3], 0.0)
                    else:
                        ii = e.tensor_copy(out=xm[:, :, 0:3], in_=halo[:])
                    vE.inc(ii)

                def mk_win(ob):
                    def em(e, S=S, prev=prev, ob=ob):
                        if ob < 2:
                            W(e, ("xzmm", prev, 31))
                        elif ob < 10:
                            W(e, ("xzmm", S, 2 * (ob - 2) + 1))
                        else:
                            W(e, ("xzmm", S, 2 * ob - 3))
                        dW[ob % 2].inc(e.dma_start(
                            out=win_b[ob % 2][:],
                            in_=w_in.rearrange("(g p) o -> p g o", p=P)[:, :, ob * 128:(ob + 1) * 128]))
                    return em

                def _xzidx(k):
                    if k < 16:
                        return k // 2, k % 2, k % 2
                    return 8 + (k - 16) // 2, (k - 16) % 2, 0

                def mk_xz(k):
                    ob, th, bank = _xzidx(k)
                    def em(e, S=S, prev=prev, k=k, ob=ob, th=th, bank=bank):
                        if k == 0:
                            W(e, ("xld", S, 1))
                            W(e, ("halo", S))
                            W(e, ("xzev", prev, 31))
                        if k == 1:
                            W(e, ("xzev", prev, 15))
                            W(e, ("lnvar", prev - 2, 3))
                        W(e, ("win", S, ob))
                        if 2 <= k < 16:
                            W(e, ("xzev", S, k - 2))
                        elif k == 16:
                            W(e, ("xzev", S, 14))
                        elif k > 16:
                            W(e, ("xzev", S, k - 1))
                        for g in range(8):
                            ii = e.matmul(pp[bank][:, :], win_b[ob % 2][:, g, :],
                                          x_bf[:, g, th * 512:(th + 1) * 512],
                                          start=(g == 0), stop=(g == 7))
                        tE.inc(ii)
                    return em

                def mk_xzev(k):
                    ob, th, bank = _xzidx(k)
                    def em(e, S=S, prev=prev, k=k, ob=ob, th=th, bank=bank):
                        W(e, ("xzmm", S, k))
                        if k == 0:
                            W(e, ("convmm", prev, 15))
                        if ob < 8:
                            ii = e.copy(out=xm[:, ob, 3 + th * 512: 3 + (th + 1) * 512],
                                        in_=pp[bank][:, :])
                        else:
                            W(e, ("gate", prev, ob - 8))
                            ii = e.activation(out=silu_z[:, ob - 8, th * 512:(th + 1) * 512],
                                              in_=pp[bank][:, :], func=AF.Silu)
                        sE.inc(ii)
                    return em

                def em_hsave(e, S=S):
                    W(e, ("xzev", S, 15))
                    vE.inc(e.tensor_copy(out=halo[:], in_=xm[:, :, HQ:HQ + 3]))

                def mk_conv(c):
                    g, th = c // 2, c % 2
                    def em(e, S=S, prev=prev, c=c, g=g, th=th):
                        W(e, ("xzev", S, 2 * g + 1))
                        if c == 0:
                            W(e, ("projev", prev, 1))
                        else:
                            W(e, ("convev", S, c - 1))
                        for j in range(4):
                            ii = e.matmul(pp[2][:, :], dg_sb[:, g, j, :],
                                          xm[:, g, th * 512 + j: th * 512 + j + 512],
                                          start=(j == 0), stop=(j == 3))
                        tE.inc(ii)
                    return em

                def mk_convev(c):
                    g, th = c // 2, c % 2
                    def em(e, S=S, prev=prev, c=c, g=g, th=th):
                        W(e, ("convmm", S, c))
                        W(e, ("ic", prev, g * 16 + 15))
                        W(e, ("z", prev, g * 16 + 15))
                        ii = e.activation(out=x_conv[:, g, th * 512:(th + 1) * 512],
                                          in_=pp[2][:, :], func=AF.Silu,
                                          bias=cb_sb[:, g:g + 1], scale=1.0)
                        sE.inc(ii)
                    return em

                def mk_proj(th):
                    def em(e, S=S, prev=prev, th=th):
                        W(e, ("convev", S, 15))
                        if th == 1:
                            W(e, ("projev", S, 0))
                        for g in range(8):
                            ii = e.matmul(pp[2][0:65, :], wxp_sb[:, g, :],
                                          x_conv[:, g, th * 512:(th + 1) * 512],
                                          start=(g == 0), stop=(g == 7))
                        tE.inc(ii)
                    return em

                def mk_projev(th):
                    def em(e, S=S, prev=prev, th=th):
                        W(e, ("projmm", S, th))
                        if th == 0:
                            W(e, ("bcv", prev))
                            W(e, ("scrc", prev))
                        sl = slice(th * 512, (th + 1) * 512)
                        e.copy(out=b_bf[:, sl], in_=pp[2][0:16, :])
                        e.copy(out=c_bf[:, sl], in_=pp[2][32:48, :])
                        ii = e.activation(out=erow[:, sl], in_=pp[2][64:65, :], func=AF.Exp)
                        sE.inc(ii)
                        if th == 1:
                            ii = e.activation(out=erow[:], in_=erow[:], func=AF.Ln, bias=1.0)
                            sE.inc(ii)
                    return em

                def em_dscr(e, S=S, prev=prev, b=b, q=q):
                    W(e, ("drow", S))
                    W(e, ("da", prev, 127))
                    W(e, ("bcv", prev))
                    dD.inc(e.dma_start(out=scr_d[b, q], in_=erow[:]))
                    W(e, ("dscrw", S))
                    dD.inc(e.dma_start(
                        out=dlb[:],
                        in_=scr_d[b, q].rearrange("a t -> (a t)").partition_broadcast(P)))
                    dD.inc(e.dma_start(
                        out=bd16[:],
                        in_=scr_d[b, q].rearrange("a t -> (a t)").partition_broadcast(NST)))

                def em_bcv(e, S=S):
                    W(e, ("bd16", S))
                    vE.inc(e.tensor_tensor(out=bdel_bf[:], in0=b_bf[:], in1=bd16[:],
                                           op=ALU.mult))

                def em_scrw(e, S=S, b=b, q=q):
                    W(e, ("bcv", S))
                    dSC.inc(e.dma_start(out=scr_b[b, q], in_=bdel_bf[:]))
                    dSC.inc(e.dma_start(out=scr_c[b, q], in_=c_bf[:]))

                fr = {
                    "xld": [("G", [("X", ("xld", S, h))], mk_xld(h)) for h in range(2)],
                    "win": [("G", [(f"W{ob % 2}", ("win", S, ob))], mk_win(ob)) for ob in range(16)],
                    "halo": ("V", [("V", ("halo", S))], em_halo),
                    "xz": [("T", [("T", ("xzmm", S, k))], mk_xz(k)) for k in range(32)],
                    "xzev": [("S", [("S", ("xzev", S, k))], mk_xzev(k)) for k in range(32)],
                    "hsave": ("V", [("V", ("hsave", S))], em_hsave),
                    "conv": [("T", [("T", ("convmm", S, c))], mk_conv(c)) for c in range(16)],
                    "convev": [("S", [("S", ("convev", S, c))], mk_convev(c)) for c in range(16)],
                    "proj": [("T", [("T", ("projmm", S, th))], mk_proj(th)) for th in range(2)],
                    "projev": [
                        ("S", [("S", ("projev", S, 0))], mk_projev(0)),
                        ("S", [("S", ("projev", S, 1)), ("S", ("drow", S))], mk_projev(1))],
                    "dscr": ("Y", [("DD", ("dscrw", S)), ("DD", ("dlb", S)), ("DD", ("bd16", S))], em_dscr),
                    "bcv": ("V", [("V", ("bcv", S))], em_bcv),
                    "scrw": ("Y", [("SC", ("scrb", S)), ("SC", ("scrc", S))], em_scrw),
                }
                return fr

            def plan_wout(S, nxt):
                b, q = steps[S]
                tok0 = q * HQ
                units = {"wo": [], "xres": [], "mm": [], "res": [], "cast": [],
                         "stat": [], "fin": [], "xh": []}

                for w in range(32):
                    th, ob = w // 8, w % 8

                    def em_wo(e, S=S, w=w, th=th, ob=ob):
                        if w >= 2:
                            W(e, ("woutmm", S, w - 2))
                        dO[w % 2].inc(e.dma_start(
                            out=wo_b[w % 2][:],
                            in_=w_out.rearrange("(g p) o -> p g o", p=P)[:, :, ob * 128:(ob + 1) * 128]))
                    units["wo"].append(("G", [(f"O{w % 2}", ("wo", S, w))], em_wo))

                    def em_mm(e, S=S, nxt=nxt, w=w, th=th, ob=ob, tok0=tok0):
                        if w == 0:
                            W(e, ("gate", S, 7))
                            W(e, ("woutres", S - 1, 31))
                        else:
                            W(e, ("woutres", S, w - 1))
                        W(e, ("wo", S, w))
                        for g in range(8):
                            ii = e.matmul(pp[7][:, 0:TH], wo_b[w % 2][:, g, :],
                                          t3[S % 2][:, g, th * TH:(th + 1) * TH],
                                          start=(g == 0), stop=(g == 7))
                        tE.inc(ii)
                    units["mm"].append(("T", [("T", ("woutmm", S, w))], em_mm))

                    def em_res(e, S=S, w=w, th=th, ob=ob):
                        if ob == 0:
                            W(e, ("xres", S, th))
                        if th > 0:
                            W(e, ("lncast", S, th - 1, ob))
                        else:
                            W(e, ("lncast", S - 1, 3, ob))
                        W(e, ("woutmm", S, w))
                        ii = e.tensor_tensor(out=res[:, ob, :], in0=pp[7][:, 0:TH],
                                             in1=xres[:, ob, :], op=ALU.add)
                        vE.inc(ii)
                    units["res"].append(("V", [("V", ("woutres", S, w))], em_res))

                    def em_cast(e, S=S, w=w, th=th, ob=ob):
                        W(e, ("woutres", S, w))
                        if ob >= 2:
                            W(e, ("lnstat", S, th, ob - 2))
                        elif th > 0:
                            W(e, ("lnstat", S, th - 1, 6 + ob))
                        else:
                            W(e, ("lnstat", S - 1, 3, 6 + ob))
                        e.copy(out=rsbf[:, ob % 2, 0:TH], in_=res[:, ob, :])
                        ii = e.activation(out=rsbf[:, ob % 2, TH:2 * TH], in_=res[:, ob, :],
                                          func=AF.Square)
                        sE.inc(ii)
                    units["cast"].append(("S", [("S", ("lncast", S, th, ob))], em_cast))

                    def em_stat(e, S=S, w=w, th=th, ob=ob):
                        if ob == 0:
                            if th == 0:
                                W(e, ("xzev", S + 2, 15))
                                W(e, ("lnvar", S - 1, 3))
                            else:
                                W(e, ("lnm", S, th - 1))
                                W(e, ("lnvar", S, th - 1))
                        W(e, ("lncast", S, th, ob))
                        ii = e.matmul(pp[1][0:1, 0:512], ones_sb[:], rsbf[:, ob % 2, :],
                                      start=(ob == 0), stop=(ob == 7),
                                      skip_group_check=True)
                        tE.inc(ii)
                    units["stat"].append(("T", [("T", ("lnstat", S, th, ob))], em_stat))

                for th in range(4):
                    def em_xres(e, S=S, b=b, th=th, tok0=tok0):
                        if th == 0:
                            W(e, ("woutres", S - 1, 31))
                        else:
                            W(e, ("woutres", S, 8 * (th - 1) + 7))
                        dR.inc(e.dma_start(
                            out=xres[:],
                            in_=x_fm[b].rearrange("(g p) t -> p g t", p=P)[
                                :, :, tok0 + th * TH:tok0 + (th + 1) * TH]))
                    units["xres"].append(("Y", [("R", ("xres", S, th))], em_xres))

                    def em_lnm(e, S=S, th=th):
                        W(e, ("lnstat", S, th, 7))
                        W(e, ("rowsbc", S, th - 1) if th > 0 else ("rowsbc", S - 1, 3))
                        e.copy(out=mrow[:], in_=pp[1][0:1, 0:256])
                        ii = e.activation(out=srow[:], in_=pp[1][0:1, 0:256], func=AF.Square)
                        sE.inc(ii)
                    def em_lnvar(e, S=S, th=th):
                        W(e, ("lnm", S, th))
                        ii = e.tensor_tensor(out=srow[:], in0=pp[1][0:1, 256:512],
                                             in1=srow[:], op=ALU.subtract)
                        vE.inc(ii)
                    def em_lnrstd(e, S=S, th=th):
                        W(e, ("lnvar", S, th))
                        e.activation(out=srow[:], in_=srow[:], func=AF.Ln,
                                     bias=eps_sb[0:1, 0:1])
                        ii = e.activation(out=srow[:], in_=srow[:], func=AF.Exp,
                                          scale=negh_sb[0:1, 0:1])
                        sE.inc(ii)
                    def em_lnmr(e, S=S, th=th):
                        W(e, ("lnrstd", S, th))
                        ii = e.tensor_tensor(out=mrow[:], in0=mrow[:],
                                             in1=srow[:], op=ALU.mult)
                        vE.inc(ii)
                    def em_rows(e, S=S, th=th):
                        W(e, ("lnmr", S, th))
                        dRW.inc(e.dma_start(out=row_scr[0:1, 0:TH], in_=mrow[:]))
                        dRW.inc(e.dma_start(out=row_scr[0:1, TH:2 * TH], in_=srow[:]))
                        W(e, ("roww", S, th))
                        dRW.inc(e.dma_start(out=mrb[:],
                                    in_=row_scr[0:1, 0:TH].rearrange("a t -> (a t)")
                                    .partition_broadcast(P)))
                        dRW.inc(e.dma_start(out=rsb[:],
                                    in_=row_scr[0:1, TH:2 * TH].rearrange("a t -> (a t)")
                                    .partition_broadcast(P)))
                    def em_xh(e, S=S, th=th):
                        W(e, ("rowsbc", S, th))
                        if th == 0:
                            W(e, ("xhstore", S - 1, 3))
                        else:
                            W(e, ("xhstore", S, th - 1))
                        for ob in range(8):
                            e.tensor_tensor(out=res[:, ob, :], in0=res[:, ob, :], in1=mrb[:],
                                            op=ALU.subtract)
                            ii = e.tensor_tensor(out=xho[:, ob, :], in0=res[:, ob, :],
                                                 in1=rsb[:], op=ALU.mult)
                            vE.inc(ii)
                    def em_xhst(e, S=S, b=b, th=th, tok0=tok0):
                        W(e, ("lnxh", S, th))
                        dH.inc(e.dma_start(
                            out=xhat_scr[b].rearrange("(g p) t -> p g t", p=P)[
                                :, :, tok0 + th * TH:tok0 + (th + 1) * TH],
                            in_=xho[:]))
                    units["fin"].append([
                        ("S", [("S", ("lnm", S, th))], em_lnm),
                        ("V", [("V", ("lnvar", S, th))], em_lnvar),
                        ("S", [("S", ("lnrstd", S, th))], em_lnrstd),
                        ("V", [("V", ("lnmr", S, th))], em_lnmr),
                        ("Y", [("RW", None), ("RW", ("roww", S, th)), ("RW", None),
                               ("RW", ("rowsbc", S, th))], em_rows),
                    ])
                    units["xh"].append([
                        ("V", [("V", None)] * 7 + [("V", ("lnxh", S, th))], em_xh),
                        ("Y", [("H", ("xhstore", S, th))], em_xhst),
                    ])
                return units

            def plan_scan(S):
                b, q = steps[S]

                def mk_bb(blk):
                    nbb = blk % 8
                    def em(e, S=S, blk=blk, nbb=nbb):
                        if blk < 2:
                            W(e, ("scrb", S))
                            W(e, ("z", S - 1, 2 * (blk + 62) + 1))
                        else:
                            W(e, ("z", S, 2 * (blk - 2) + 1))
                        dB[nbb % 2].inc(e.dma_start(
                            out=bbv[nbb % 2],
                            in_=scr_b[b, q, nbb * 2:(nbb + 1) * 2, :]
                            .rearrange("n t -> (n t)").partition_broadcast(P)))
                    return em

                def mk_cb(blk):
                    nbb = blk % 8
                    def em(e, S=S, blk=blk, nbb=nbb):
                        if blk < 2:
                            W(e, ("scrc", S))
                            W(e, ("z", S - 1, 2 * (blk + 62) + 1))
                        else:
                            W(e, ("z", S, 2 * (blk - 2) + 1))
                        dC[nbb % 2].inc(e.dma_start(
                            out=cbv[nbb % 2],
                            in_=scr_c[b, q, nbb * 2:(nbb + 1) * 2, :]
                            .rearrange("n t -> (n t)").partition_broadcast(P)))
                    return em

                def mk_da(blk):
                    g, nbb = blk // 8, blk % 8
                    def em(e, S=S, blk=blk, g=g, nbb=nbb):
                        for j in range(2):
                            i = blk * 2 + j
                            col = g * 16 + nbb * 2 + j
                            if i >= 2:
                                W(e, ("z", S, i - 2))
                            if i == 0:
                                W(e, ("dlb", S))
                            ii = e.activation(out=da_v[i % 2], in_=dlb[:],
                                              func=AF.Exp, scale=a_sb[:, col:col + 1])
                            sE.inc(ii)
                    return em

                def mk_tiles(blk):
                    g, nbb = blk // 8, blk % 8
                    slot = nbb % 2
                    def em(e, S=S, blk=blk, g=g, nbb=nbb, slot=slot, q=q):
                        for j in range(2):
                            i = blk * 2 + j
                            col = g * 16 + nbb * 2 + j
                            k = i % 2
                            if i >= 4:
                                W(e, ("ic", S, i - 4))
                            if j == 0:
                                W(e, ("bb", S, blk))
                                W(e, ("cb", S, blk))
                            if q == 0 and i >= 2:
                                e.tensor_copy(out=hcar[:, i - 2:i - 1],
                                              in_=h_b[k][:, HQ - 1:HQ])
                            e.tensor_tensor(out=db_b[k][:], in0=x_conv[:, g, :],
                                            in1=bbv[slot][:, j, :], op=ALU.mult)
                            W(e, ("da", S, i))
                            init = 0.0 if q == 0 else hcar[:, col:col + 1]
                            ii = e.tensor_tensor_scan(
                                out=h_b[k][:], data0=da_v[i % 2], data1=db_b[k][:],
                                initial=init, op0=ALU.mult, op1=ALU.add)
                            ii = e.tensor_tensor(out=z_b[i % 4][:], in0=h_b[k][:],
                                                 in1=cbv[slot][:, j, :], op=ALU.mult)
                            vE.inc(ii)
                    incs = []
                    for j in range(2):
                        i = blk * 2 + j
                        incs.append(("V", ("z", S, i)))
                    return incs, em

                def mk_gate(g):
                    yp = (pp[3], pp[4]) if g % 2 == 0 else (pp[5], pp[6])
                    def em(e, S=S, g=g, yp=yp, q=q):
                        if g == 7 and q == 0:
                            e.tensor_copy(out=hcar[:, 126:127],
                                          in_=h_b[0][:, HQ - 1:HQ])
                            e.tensor_copy(out=hcar[:, 127:128],
                                          in_=h_b[1][:, HQ - 1:HQ])
                        W(e, ("ic", S, g * 16 + 15))
                        W(e, ("xzev", S, 17 + 2 * g))
                        W(e, ("woutmm", S - 2, 31))
                        e.tensor_tensor(out=t3[S % 2][:, g, 0:512], in0=yp[0][:, :],
                                        in1=silu_z[:, g, 0:512], op=ALU.mult)
                        ii = e.tensor_tensor(out=t3[S % 2][:, g, 512:1024], in0=yp[1][:, :],
                                             in1=silu_z[:, g, 512:1024], op=ALU.mult)
                        vE.inc(ii)
                    return em

                def mk_tid(blk):
                    g, nbb = blk // 8, blk % 8
                    yp = (pp[3], pp[4]) if g % 2 == 0 else (pp[5], pp[6])
                    def em(e, S=S, blk=blk, g=g, nbb=nbb, yp=yp):
                        if nbb == 0:
                            if g >= 2:
                                W(e, ("gate", S, g - 2))
                            else:
                                W(e, ("gate", S - 1, 6 + g))
                        for j in range(2):
                            i = blk * 2 + j
                            W(e, ("z", S, i))
                            first = (j == 0 and nbb == 0)
                            e.matmul(yp[0][:, :], i128_sb[:], z_b[i % 4][:, 0:512],
                                     start=first, stop=False)
                            ii = e.matmul(yp[1][:, :], i128_sb[:], z_b[i % 4][:, 512:1024],
                                          start=first, stop=False)
                            if j == 1 and nbb == 7:
                                e.matmul(yp[0][:, :], dd_sb[:, g, :],
                                         x_conv[:, g, 0:512], start=False, stop=True)
                                ii = e.matmul(yp[1][:, :], dd_sb[:, g, :],
                                              x_conv[:, g, 512:1024],
                                              start=False, stop=True)
                            tE.inc(ii)
                    return em

                return dict(mk_bb=mk_bb, mk_cb=mk_cb, mk_da=mk_da, mk_tiles=mk_tiles,
                            mk_gate=mk_gate, mk_tid=mk_tid)

            NSTEP = len(steps)
            fronts = {0: plan_front(0, prev=-1)}
            f0 = fronts[0]
            for u in f0["xld"]:
                op(*u)
            op(*f0["halo"])
            for ob in range(8):
                op(*f0["win"][ob])
            for k in range(16):
                op(*f0["xz"][k]); op(*f0["xzev"][k])
            op(*f0["hsave"])
            for ob in range(8, 16):
                op(*f0["win"][ob])
            for k in range(16, 32):
                op(*f0["xz"][k]); op(*f0["xzev"][k])
            for c in range(16):
                op(*f0["conv"][c]); op(*f0["convev"][c])
            for t_ in range(2):
                op(*f0["proj"][t_]); op(*f0["projev"][t_])
            op(*f0["dscr"]); op(*f0["bcv"]); op(*f0["scrw"])

            wout_units = {}
            for S in range(NSTEP):
                sc = plan_scan(S)
                if S + 1 < NSTEP:
                    fr = plan_front(S + 1, prev=S)
                    fronts[S + 1] = fr
                else:
                    fr = None
                wo = wout_units.get(S - 1)

                PLACE = {}
                def at(B, unit):
                    PLACE.setdefault(B, []).append(unit)

                at(13, ("V", [("V", ("gate", S, 0))], sc["mk_gate"](0)))
                for g in range(1, 7):
                    at(8 * g + 9, ("V", [("V", ("gate", S, g))], sc["mk_gate"](g)))

                if wo is not None:
                    for th in range(4):
                        at(15 if th == 0 else 13 + 12 * th, wo["xres"][th])
                        for ob in range(8):
                            w = 8 * th + ob
                            B0 = 16 + 12 * th + ob
                            at(B0 - 1, wo["wo"][w])
                            at(B0, wo["mm"][w])
                            at(B0 + 1, wo["res"][w])
                            at(B0 + 2, wo["cast"][w])
                            at(B0 + 3, wo["stat"][w])
                        fin = wo["fin"][th]
                        at(27 + 12 * th, fin[0]); at(27 + 12 * th, fin[1])
                        at(28 + 12 * th, fin[2]); at(28 + 12 * th, fin[3])
                        at(28 + 12 * th, fin[4])
                        at(29 + 12 * th, wo["xh"][th][0])
                        at(30 + 12 * th, wo["xh"][th][1])

                if fr is not None:
                    at(0, fr["halo"])
                    at(0, fr["win"][0]); at(0, fr["win"][1])
                    at(1, fr["xld"][0]); at(3, fr["xld"][1])
                    for ob in range(2, 8):
                        at(2 * ob + 2, fr["win"][ob])
                    for k in range(16):
                        at(3 + k, fr["xz"][k])
                        at(4 + k, fr["xzev"][k])
                    at(20, fr["hsave"])
                    for ob in range(8, 16):
                        at(20 + 6 * (ob - 8), fr["win"][ob])
                    for k in range(16, 32):
                        gp, th2 = (k - 16) // 2, (k - 16) % 2
                        at(21 + 6 * gp + 2 * th2, fr["xz"][k])
                        at(22 + 6 * gp + 2 * th2, fr["xzev"][k])
                    lastev = 20
                    for c in range(16):
                        bcv_ = max(21 + 2 * c, lastev + 1)
                        bev_ = max(bcv_ + 1, 8 * (c // 2) + 8)
                        at(bcv_, fr["conv"][c])
                        at(bev_, fr["convev"][c])
                        lastev = bev_

                for blk in range(64):
                    op("G", [(f"B{blk % 2}", ("bb", S, blk))], sc["mk_bb"](blk))
                    op("Y", [(f"C{blk % 2}", ("cb", S, blk))], sc["mk_cb"](blk))
                    op("S", [("S", ("da", S, blk * 2 + j)) for j in range(2)],
                       sc["mk_da"](blk))
                    incs, em = sc["mk_tiles"](blk)
                    op("V", incs, em)
                    op("T", [("T", ("ic", S, blk * 2 + j)) for j in range(2)],
                       sc["mk_tid"](blk))
                    for u in PLACE.get(blk, []):
                        op(*u)
                op("V", [("V", ("gate", S, 7))], sc["mk_gate"](7))
                for B in sorted(k2 for k2 in PLACE if k2 >= 64):
                    for u in PLACE[B]:
                        op(*u)
                if fr is not None:
                    for t_ in range(2):
                        op(*fr["proj"][t_]); op(*fr["projev"][t_])
                    op(*fr["dscr"]); op(*fr["bcv"]); op(*fr["scrw"])
                wout_units[S] = plan_wout(S, nxt=S + 1)

            wo = wout_units[NSTEP - 1]
            for th in range(4):
                op(*wo["xres"][th])
                for ob in range(8):
                    w = th * 8 + ob
                    op(*wo["wo"][w]); op(*wo["mm"][w])
                    op(*wo["res"][w]); op(*wo["cast"][w]); op(*wo["stat"][w])
                for u in wo["fin"][th]:
                    op(*u)
                for u in wo["xh"][th]:
                    op(*u)

            sim = {k: c.n for k, c in CTRS.items()}
            for eng, incs, _ in PLAN:
                for cname, label in incs:
                    sim[cname] += CTRS[cname].step
                    if label is not None:
                        C[label] = (cname, sim[cname])

            for eng_name, sect in (("G", block.gpsimd), ("Y", block.sync),
                                   ("S", block.scalar), ("V", block.vector),
                                   ("T", block.tensor)):
                fns = [em for e2, _, em in PLAN if e2 == eng_name]
                if fns:
                    sect(lambda e, fns=fns: [f(e) for f in fns])

            for k, c in CTRS.items():
                assert c.n == sim[k], (k, c.n, sim[k])

            barrier()

        with contextlib.ExitStack() as s2s:
            e2 = s2s.enter_context
            w1_sb = e2(nc.sbuf_tensor([P, 8, DF], F8E5))
            w2_sb = e2(nc.sbuf_tensor([P, 32, D], F8E5))
            xh_b = [e2(nc.sbuf_tensor(f"xh{i}", [P, 8, S2], F32)) for i in range(2)]
            xh8 = e2(nc.sbuf_tensor([P, 8, S2], F8E5))
            hf8 = e2(nc.sbuf_tensor([P, 32, S2], F8E5))
            msk_b = [e2(nc.sbuf_tensor(f"msk{i}", [P, 8, S2], BF16)) for i in range(2)]
            dr2 = e2(nc.sbuf_tensor([P, 8, S2], F32))
            rs2 = e2(nc.sbuf_tensor([P, 8, S2], F32))
            rbf2 = e2(nc.sbuf_tensor([P, 2, S2], BF16))
            sbf2 = e2(nc.sbuf_tensor([P, 2, S2], BF16))
            mrow2 = e2(nc.sbuf_tensor([1, 2 * S2], F32))
            srow2 = e2(nc.sbuf_tensor([1, S2], F32))
            mrb2 = e2(nc.sbuf_tensor([P, S2], F32))
            rsb2 = e2(nc.sbuf_tensor([P, S2], F32))

            def g2_(g):
                dG.inc(g.dma_start(out=w1_sb[:],
                                   in_=w1t.rearrange("(g p) n -> p g n", p=P)))
                dG.inc(g.dma_start(out=w2_sb[:],
                                   in_=w2t.rearrange("(k p) o -> p k o", p=P)))
            block.gpsimd(g2_)
            w_d = dG.n

            prev_f1 = None
            prev_f2 = None

            slices = [(b, si) for b in range(NB) for si in range(T // S2)]
            for gi, (b, si) in enumerate(slices):
                tok = si * S2
                bi = gi % 2
                D0, G0, S0, V0, T0 = dS.n, dG.n, sE.n, vE.n, tE.n
                VX = V0 + 1
                VD, VV2, VR2 = V0 + 2, V0 + 3, V0 + 4
                VO2 = V0 + 12

                def gl_(gq, b=b, tok=tok, bi=bi, gi=gi):
                    if gi >= 2:
                        gq.wait_ge(vE.sem, vd_hist[gi - 2])
                    dG.inc(gq.dma_start(
                        out=xh_b[bi][:],
                        in_=xhat_scr[b].rearrange("(g p) t -> p g t", p=P)[:, :, tok:tok + S2]))
                    dG.inc(gq.dma_start(
                        out=msk_b[bi][:],
                        in_=mask_fm[b].rearrange("(g p) t -> p g t", p=P)[:, :, tok:tok + S2]))
                block.gpsimd(gl_)

                def y_(y, b=b, tok=tok, bi=bi, gi=gi):
                    y.wait_ge(vE.sem, VR2)
                    dS.inc(y.dma_start(out=row_scr2[0:1, 0:S2], in_=mrow2[:, 0:S2]))
                    dS.inc(y.dma_start(out=row_scr2[0:1, S2:2 * S2], in_=srow2[:]))
                    y.wait_ge(dS.sem, dS.n)
                    dS.inc(y.dma_start(out=mrb2[:],
                                in_=row_scr2[0:1, 0:S2].rearrange("a t -> (a t)")
                                .partition_broadcast(P)))
                    dS.inc(y.dma_start(out=rsb2[:],
                                in_=row_scr2[0:1, S2:2 * S2].rearrange("a t -> (a t)")
                                .partition_broadcast(P)))
                    y.wait_ge(vE.sem, VO2)
                    dS.inc(y.dma_start(
                        out=out_fm[b].rearrange("(g p) t -> p g t", p=P)[:, :, tok:tok + S2],
                        in_=dr2[:]))
                block.sync(y_)
                ld_d = G0 + 32
                rows_bc = D0 + 64
                vd_hist = vd_hist if gi else {}

                def tt_(t, gi=gi, S0=S0, V0=V0):
                    t.wait_ge(vE.sem, VX)
                    if gi == 0:
                        t.wait_ge(dG.sem, w_d)
                    for nb2 in range(32):
                        if nb2 >= 2:
                            t.wait_ge(sE.sem, S0 + nb2 - 1)
                        elif gi > 0:
                            t.wait_ge(sE.sem, S0 - 19 + nb2)
                        for g in range(8):
                            ii = t.matmul(pp[nb2 % 2][:, 0:S2],
                                          w1_sb[:, g, nb2 * 128:(nb2 + 1) * 128],
                                          xh8[:, g, :], start=(g == 0), stop=(g == 7))
                        tE.inc(ii)
                    t.wait_ge(sE.sem, S0 + 32)
                    for ob in range(8):
                        if ob >= 2:
                            t.wait_ge(sE.sem, S0 + 32 + ob - 1)
                        for kb in range(32):
                            ii = t.matmul(pp[2 + ob % 2][:, 0:S2],
                                          w2_sb[:, kb, ob * 128:(ob + 1) * 128],
                                          hf8[:, kb, :], start=(kb == 0), stop=(kb == 31))
                        tE.inc(ii)
                    for ob in range(8):
                        t.wait_ge(sE.sem, S0 + 40 + ob + 1)
                        t.matmul(pp[4][0:1, 0:S2], ones_sb[:], rbf2[:, ob % 2, :],
                                 start=(ob == 0), stop=(ob == 7), skip_group_check=True)
                        ii = t.matmul(pp[5][0:1, 0:S2], ones_sb[:], sbf2[:, ob % 2, :],
                                      start=(ob == 0), stop=(ob == 7), skip_group_check=True)
                        tE.inc(ii)
                block.tensor(tt_)

                def ss_(s, T0=T0, V0=V0, D0=D0, prev_f2=prev_f2):
                    if prev_f2 is not None:
                        s.wait_ge(tE.sem, prev_f2)
                    for nb2 in range(32):
                        s.wait_ge(tE.sem, T0 + nb2 + 1)
                        ii = s.activation(out=hf8[:, nb2, :], in_=pp[nb2 % 2][:, 0:S2],
                                          func=AF.Relu, bias=b1q_sb[:, nb2:nb2 + 1],
                                          scale=1.0)
                        sE.inc(ii)
                    for ob in range(8):
                        s.wait_ge(tE.sem, T0 + 32 + ob + 1)
                        ii = s.activation(out=dr2[:, ob, :], in_=pp[2 + ob % 2][:, 0:S2],
                                          func=AF.Identity, bias=b2q_sb[:, ob:ob + 1],
                                          scale=1.0)
                        sE.inc(ii)
                    s.wait_ge(vE.sem, VD)
                    for ob in range(8):
                        if ob >= 2:
                            s.wait_ge(tE.sem, T0 + 40 + ob - 1)
                        s.copy(out=rbf2[:, ob % 2, :], in_=rs2[:, ob, :])
                        ii = s.activation(out=sbf2[:, ob % 2, :], in_=rs2[:, ob, :],
                                          func=AF.Square)
                        sE.inc(ii)
                    s.wait_ge(tE.sem, T0 + 48)
                    s.wait_ge(dS.sem, D0)
                    s.copy(out=mrow2[:, 0:S2], in_=pp[4][0:1, 0:S2])
                    ii = s.activation(out=srow2[:], in_=pp[4][0:1, 0:S2], func=AF.Square)
                    sE.inc(ii)
                    s.wait_ge(vE.sem, VV2)
                    s.activation(out=mrow2[:, S2:2 * S2], in_=srow2[:], func=AF.Ln,
                                 bias=eps_sb[0:1, 0:1])
                    ii = s.activation(out=srow2[:], in_=mrow2[:, S2:2 * S2], func=AF.Exp,
                                      scale=negh_sb[0:1, 0:1])
                    sE.inc(ii)
                block.scalar(ss_)

                def vv_(v, bi=bi, gi=gi, S0=S0, T0=T0, D0=D0, ld_d=ld_d,
                        rows_bc=rows_bc, prev_f2=prev_f1):
                    v.wait_ge(dG.sem, ld_d)
                    v.wait_ge(dS.sem, D0)
                    if gi == 0:
                        v.wait_ge(dG.sem, w_d)
                    if prev_f2 is not None:
                        v.wait_ge(tE.sem, prev_f2)
                    ii = v.tensor_copy(
                        out=xh8[:].rearrange("p a t -> p (a t)"),
                        in_=xh_b[bi][:].rearrange("p a t -> p (a t)"))
                    vE.inc(ii)
                    v.wait_ge(sE.sem, S0 + 40)
                    v.tensor_tensor(out=dr2[:].rearrange("p a t -> p (a t)"),
                                    in0=dr2[:].rearrange("p a t -> p (a t)"),
                                    in1=msk_b[bi][:].rearrange("p a t -> p (a t)"),
                                    op=ALU.mult)
                    ii = v.tensor_tensor(out=rs2[:].rearrange("p a t -> p (a t)"),
                                         in0=dr2[:].rearrange("p a t -> p (a t)"),
                                         in1=xh_b[bi][:].rearrange("p a t -> p (a t)"),
                                         op=ALU.add)
                    vE.inc(ii)
                    v.wait_ge(sE.sem, S0 + 49)
                    ii = v.tensor_tensor(out=srow2[:], in0=pp[5][0:1, 0:S2], in1=srow2[:],
                                         op=ALU.subtract)
                    vE.inc(ii)
                    v.wait_ge(sE.sem, S0 + 50)
                    ii = v.tensor_tensor(out=mrow2[:, 0:S2], in0=mrow2[:, 0:S2],
                                         in1=srow2[:], op=ALU.mult)
                    vE.inc(ii)
                    v.wait_ge(dS.sem, rows_bc)
                    for ob in range(8):
                        v.tensor_tensor(out=rs2[:, ob, :], in0=rs2[:, ob, :], in1=mrb2[:],
                                        op=ALU.subtract)
                        ii = v.tensor_tensor(out=dr2[:, ob, :], in0=rs2[:, ob, :],
                                             in1=rsb2[:], op=ALU.mult)
                        vE.inc(ii)
                block.vector(vv_)
                vd_hist[gi] = VD
                prev_f1 = T0 + 32
                prev_f2 = T0 + 40
                assert vE.n == VO2 and sE.n == S0 + 50 and tE.n == T0 + 48
                assert dS.n == D0 + 80 and dG.n == G0 + 32

            def yfin_(y):
                y.wait_ge(dS.sem, dS.n)
                y.wait_ge(dG.sem, dG.n)
            block.sync(yfin_)

    return nc


_NC_CACHE = {}


def kernel(L0_attn_x, L0_attn_W_in, L0_attn_conv_w, L0_attn_conv_b,
           L0_attn_W_xproj, L0_attn_log_A, L0_attn_D_param, L0_attn_W_out,
           L0_ffn_W1, L0_ffn_b1, L0_ffn_W2, L0_ffn_b2):
    x = np.asarray(L0_attn_x, np.float32)
    w_in = np.ascontiguousarray(np.asarray(L0_attn_W_in, np.float32))
    conv_w = np.asarray(L0_attn_conv_w, np.float32)
    conv_b = np.asarray(L0_attn_conv_b, np.float32)
    w_xproj = np.asarray(L0_attn_W_xproj, np.float32)
    log_a = np.asarray(L0_attn_log_A, np.float32)
    d_param = np.asarray(L0_attn_D_param, np.float32)
    w_out = np.ascontiguousarray(np.asarray(L0_attn_W_out, np.float32))
    w1 = np.asarray(L0_ffn_W1, np.float32)
    b1 = np.asarray(L0_ffn_b1, np.float32)
    w2 = np.asarray(L0_ffn_W2, np.float32)
    b2 = np.asarray(L0_ffn_b2, np.float32)

    wxp_pad = np.zeros((D, 65), np.float32)
    wxp_pad[:, 0:16] = w_xproj[:, 0:16]
    wxp_pad[:, 32:48] = w_xproj[:, 16:32]
    wxp_pad[:, 64] = w_xproj[:, 32]
    w1t = np.ascontiguousarray(w1.T)
    w2t = np.ascontiguousarray(w2.T)
    loga_rr = np.ascontiguousarray(
        log_a.reshape(8, P, NST).transpose(1, 0, 2).reshape(P, 128))
    dgconv = np.zeros((8, 4, P, P), np.float32)
    for g in range(8):
        for j in range(4):
            np.fill_diagonal(dgconv[g, j], conv_w[g * P:(g + 1) * P, 0, j])
    dgd = np.zeros((8, P, P), np.float32)
    for g in range(8):
        np.fill_diagonal(dgd[g], d_param[g * P:(g + 1) * P])
    cbcol = np.ascontiguousarray(conv_b.reshape(8, P).T)
    b1rr = np.ascontiguousarray(b1.reshape(32, P).T)
    b2rr = np.ascontiguousarray(b2.reshape(8, P).T)
    i128a = np.eye(P, dtype=ml_dtypes.bfloat16)
    onesa = np.full((P, 1), 1.0 / 1024.0, dtype=ml_dtypes.bfloat16)

    import jax
    with jax.default_device(jax.devices("cpu")[0]):
        try:
            key42 = jax.random.key(42, impl="threefry2x32")
        except TypeError:
            key42 = jax.random.key(42)
        keep = np.asarray(jax.random.bernoulli(key42, 0.8, (16, T, D)))
    maskv = np.where(keep, np.float32(1.25), np.float32(0.0)).astype(ml_dtypes.bfloat16)

    if "nc" not in _NC_CACHE:
        _NC_CACHE["nc"] = build_program()
    nc = _NC_CACHE["nc"]

    in_maps = []
    for c in range(NCORE):
        xs = np.ascontiguousarray(x[2 * c:2 * c + 2].transpose(0, 2, 1))
        ms = np.ascontiguousarray(maskv[2 * c:2 * c + 2].transpose(0, 2, 1))
        in_maps.append(dict(
            x_fm=xs, w_in=w_in, w_out=w_out, wxp=wxp_pad, w1t=w1t, w2t=w2t,
            loga_r=loga_rr, dgconv=dgconv, dgd=dgd, cbcol=cbcol, b1r=b1rr,
            b2r=b2rr, mask_fm=ms, i128=i128a, onesln=onesa,
        ))

    res = run_bass_kernel_spmd(nc, in_maps, core_ids=list(range(NCORE)))
    outs = [np.asarray(res.results[c]["out_fm"]).transpose(0, 2, 1)
            for c in range(NCORE)]
    return np.ascontiguousarray(np.concatenate(outs, axis=0))


# version 15
# speedup vs baseline: 1.1156x; 1.1156x over previous
import contextlib

import numpy as np
import ml_dtypes

import concourse.bass as bass
import concourse.mybir as mybir
from concourse.bass_utils import run_bass_kernel_spmd

F32 = mybir.dt.float32
BF16 = mybir.dt.bfloat16
F8E5 = mybir.dt.float8e5
AF = mybir.ActivationFunctionType
ALU = mybir.AluOpType

P = 128
D = 1024
T = 2048
NB = 2
NST = 16
HQ = 1024
TH = 256
S2 = 512
DF = 4096
NCORE = 8
EPS = 1e-5


class Ctr:
    def __init__(self, sem, step):
        self.sem = sem
        self.n = 0
        self.step = step

    def inc(self, instr):
        instr.then_inc(self.sem, self.step)
        self.n += self.step
        return self.n


def build_program():
    nc = bass.Bass("TRN2", target_bir_lowering=False, debug=False)
    dt = nc.dram_tensor

    x_fm = dt("x_fm", [NB, D, T], F32, kind="ExternalInput").ap()
    w_in = dt("w_in", [D, 2 * D], F32, kind="ExternalInput").ap()
    w_out = dt("w_out", [D, D], F32, kind="ExternalInput").ap()
    wxp = dt("wxp", [D, 65], F32, kind="ExternalInput").ap()
    w1t = dt("w1t", [D, DF], F32, kind="ExternalInput").ap()
    w2t = dt("w2t", [DF, D], F32, kind="ExternalInput").ap()
    loga_r = dt("loga_r", [P, 128], F32, kind="ExternalInput").ap()
    dgconv = dt("dgconv", [8, 4, P, P], F32, kind="ExternalInput").ap()
    dgd = dt("dgd", [8, P, P], F32, kind="ExternalInput").ap()
    cbcol = dt("cbcol", [P, 8], F32, kind="ExternalInput").ap()
    b1r = dt("b1r", [P, 32], F32, kind="ExternalInput").ap()
    b2r = dt("b2r", [P, 8], F32, kind="ExternalInput").ap()
    mask_fm = dt("mask_fm", [NB, D, T], BF16, kind="ExternalInput").ap()
    i128 = dt("i128", [P, P], BF16, kind="ExternalInput").ap()
    onesln = dt("onesln", [P, 1], BF16, kind="ExternalInput").ap()

    out_fm = dt("out_fm", [NB, D, T], F32, kind="ExternalOutput").ap()
    xhat_scr = dt("xhat_scr", [NB, D, T], F32, kind="Internal").ap()
    scr_b = dt("scr_b", [NB, 2, NST, HQ], BF16, kind="Internal").ap()
    scr_c = dt("scr_c", [NB, 2, NST, HQ], BF16, kind="Internal").ap()
    scr_d = dt("scr_d", [NB, 2, 1, HQ], F32, kind="Internal").ap()
    row_scr = dt("row_scr", [1, 2 * TH], F32, kind="Internal").ap()
    row_scr2 = dt("row_scr2", [1, 2 * S2], F32, kind="Internal").ap()

    st = contextlib.ExitStack()
    with st:
        e = st.enter_context
        dS = Ctr(e(nc.semaphore("dS")), 16)
        dG = Ctr(e(nc.semaphore("dG")), 16)
        sE = Ctr(e(nc.semaphore("sE")), 1)
        vE = Ctr(e(nc.semaphore("vE")), 1)
        tE = Ctr(e(nc.semaphore("tE")), 1)
        dX = Ctr(e(nc.semaphore("dX")), 16)
        dW = [Ctr(e(nc.semaphore(f"dW{i}")), 16) for i in range(2)]
        dO = [Ctr(e(nc.semaphore(f"dO{i}")), 16) for i in range(2)]
        dB = [Ctr(e(nc.semaphore(f"dB{i}")), 16) for i in range(2)]
        dC = [Ctr(e(nc.semaphore(f"dC{i}")), 16) for i in range(2)]
        dR = Ctr(e(nc.semaphore("dR")), 16)
        dRW = Ctr(e(nc.semaphore("dRW")), 16)
        dD = Ctr(e(nc.semaphore("dD")), 16)
        dSC = Ctr(e(nc.semaphore("dSC")), 16)
        dH = Ctr(e(nc.semaphore("dH")), 16)
        block = e(nc.Block())

        def barrier():
            ctrs = [dS, dG, sE, vE, tE, dX, dR, dRW, dD, dSC, dH] + dW + dO + dB + dC
            waits = [(c.sem, c.n) for c in ctrs]

            def emit(eng):
                for sem, n in waits:
                    eng.wait_ge(sem, n)
            block.sync(emit)
            block.gpsimd(emit)
            block.scalar(emit)
            block.vector(emit)
            block.tensor(emit)

        a_sb = e(nc.sbuf_tensor([P, 128], F32))
        b1q_sb = e(nc.sbuf_tensor([P, 32], F32))
        b2q_sb = e(nc.sbuf_tensor([P, 8], F32))
        b1f8 = e(nc.sbuf_tensor([P, 32], F8E5))
        b2f8 = e(nc.sbuf_tensor([P, 8], F8E5))
        i128_sb = e(nc.sbuf_tensor([P, P], BF16))
        ones_sb = e(nc.sbuf_tensor([P, 1], BF16))
        eps_sb = e(nc.sbuf_tensor([P, 1], F32))
        negh_sb = e(nc.sbuf_tensor([P, 1], F32))
        pp = [e(nc.psum_tensor(f"pp{i}", [P, 512], F32)) for i in range(8)]

        with contextlib.ExitStack() as s1:
            e1 = s1.enter_context
            dg_sb = e1(nc.sbuf_tensor([P, 8, 4, P], BF16))
            dd_sb = e1(nc.sbuf_tensor([P, 8, P], BF16))
            wxp_sb = e1(nc.sbuf_tensor([P, 8, 65], BF16))
            cb_sb = e1(nc.sbuf_tensor([P, 8], F32))
            loga_sb = e1(nc.sbuf_tensor([P, 128], F32))

            x_bf = e1(nc.sbuf_tensor([P, 8, HQ], BF16))
            ubuf = e1(nc.sbuf_tensor([P, 8 * HQ], BF16))
            xm_t = e1(nc.sbuf_tensor([P, 8, HQ + 3], BF16))
            halo = e1(nc.sbuf_tensor([P, 8, 3], BF16))
            x_conv = e1(nc.sbuf_tensor([P, 8, HQ], BF16))
            silu_z = e1(nc.sbuf_tensor([P, 8, HQ], BF16))
            dlb = e1(nc.sbuf_tensor([P, HQ], F32))
            b_bf = e1(nc.sbuf_tensor([NST, HQ], BF16))
            bd16 = e1(nc.sbuf_tensor([NST, HQ], F32))
            bdel_bf = e1(nc.sbuf_tensor([NST, HQ], BF16))
            c_bf = e1(nc.sbuf_tensor([NST, HQ], BF16))
            erow = e1(nc.sbuf_tensor([1, HQ], F32))
            uf32 = e1(nc.sbuf_tensor([P, 4096], F32))
            db_b = [e1(nc.sbuf_tensor(f"db{i}", [P, HQ], BF16)) for i in range(2)]
            h_b = [e1(nc.sbuf_tensor(f"hb{i}", [P, HQ], BF16)) for i in range(2)]
            z_b = [e1(nc.sbuf_tensor(f"zb{i}", [P, HQ], BF16)) for i in range(4)]
            hcar = e1(nc.sbuf_tensor([P, 128], F32))
            t3 = [e1(nc.sbuf_tensor(f"t3{i}", [P, 8, HQ], BF16)) for i in range(2)]
            win_b = [e1(nc.sbuf_tensor(f"win{i}", [P, 8, 128], BF16)) for i in range(2)]
            wo_b = [e1(nc.sbuf_tensor(f"wo{i}", [P, 8, 128], BF16)) for i in range(2)]
            xres = e1(nc.sbuf_tensor([P, 8, TH], F32))
            xho = e1(nc.sbuf_tensor([P, 8, TH], F32))
            rsbf = e1(nc.sbuf_tensor([P, 2, 2 * TH], BF16))
            mrow = e1(nc.sbuf_tensor([1, TH], F32))
            srow = e1(nc.sbuf_tensor([1, TH], F32))
            mrb = e1(nc.sbuf_tensor([P, TH], F32))
            rsb = e1(nc.sbuf_tensor([P, TH], F32))

            xm = xm_t[:]
            bbv = [ubuf[:, s * 2 * HQ:(s + 1) * 2 * HQ]
                   .rearrange("p (n t) -> p n t", n=2) for s in range(2)]
            cbv = [ubuf[:, (4 + 2 * s) * HQ:(6 + 2 * s) * HQ]
                   .rearrange("p (n t) -> p n t", n=2) for s in range(2)]
            res = uf32[:, 2048:4096].rearrange("p (g t) -> p g t", g=8)
            da_v = [uf32[:, 0:HQ], uf32[:, HQ:2 * HQ]]

            def g_(g):
                dG.inc(g.dma_start(out=dg_sb[:],
                                   in_=dgconv.rearrange("g j k p -> k g j p")))
                dG.inc(g.dma_start(out=dd_sb[:],
                                   in_=dgd.rearrange("g k p -> k g p")))
                dG.inc(g.dma_start(out=wxp_sb[:],
                                   in_=wxp.rearrange("(g p) o -> p g o", p=P)))
                dG.inc(g.dma_start(out=b1f8[:], in_=b1r))
                dG.inc(g.dma_start(out=b2f8[:], in_=b2r))
            block.gpsimd(g_)

            def y0_(y):
                dS.inc(y.dma_start(out=loga_sb[:], in_=loga_r))
                dS.inc(y.dma_start(out=cb_sb[:], in_=cbcol))
                dS.inc(y.dma_start(out=i128_sb[:], in_=i128))
                dS.inc(y.dma_start(out=ones_sb[:], in_=onesln))
            block.sync(y0_)

            def s0_(s):
                s.wait_ge(dS.sem, dS.n)
                s.wait_ge(dG.sem, dG.n)
                s.activation(out=a_sb[:], in_=loga_sb[:], func=AF.Exp)
                s.copy(out=b1q_sb[:], in_=b1f8[:])
                ii = s.copy(out=b2q_sb[:], in_=b2f8[:])
                sE.inc(ii)
            block.scalar(s0_)

            def v0_(v):
                v.wait_ge(sE.sem, sE.n)
                v.memset(eps_sb[:], EPS)
                v.memset(negh_sb[:], -0.5)
                ii = v.tensor_scalar_mul(out=a_sb[:], in0=a_sb[:], scalar1=-1.0)
                vE.inc(ii)
            block.vector(v0_)

            PLAN = []
            C = {}
            CTRS = {"S": sE, "V": vE, "T": tE, "Y": dS, "G": dG,
                    "X": dX, "W0": dW[0], "W1": dW[1], "O0": dO[0], "O1": dO[1],
                    "B0": dB[0], "B1": dB[1], "C0": dC[0], "C1": dC[1],
                    "R": dR, "RW": dRW, "DD": dD, "SC": dSC, "H": dH}

            def op(eng, incs, emitter):
                PLAN.append((eng, incs, emitter))

            def W(e, label):
                if label in C:
                    cname, cnt = C[label]
                    e.wait_ge(CTRS[cname].sem, cnt)
                else:
                    assert not (0 <= label[1] < 4), ("missing label", label)

            steps = [(b, q) for b in range(NB) for q in range(2)]

            def plan_front(S, prev):
                b, q = steps[S]
                tok0 = q * HQ

                def mk_xld(half):
                    def em(e, S=S, prev=prev, b=b, tok0=tok0, half=half):
                        W(e, ("xzmm", prev, 31))
                        dX.inc(e.dma_start(
                            out=x_bf[:, half * 4:(half + 1) * 4, :],
                            in_=x_fm[b].rearrange("(g p) t -> p g t", p=P)[
                                :, half * 4:(half + 1) * 4, tok0:tok0 + HQ]))
                    return em

                def em_halo(e, S=S, prev=prev, q=q):
                    W(e, ("convmm", prev, 15))
                    if q == 0:
                        ii = e.memset(xm[:, :, 0:3], 0.0)
                    else:
                        ii = e.tensor_copy(out=xm[:, :, 0:3], in_=halo[:])
                    vE.inc(ii)

                def mk_win(ob):
                    def em(e, S=S, prev=prev, ob=ob):
                        if ob < 2:
                            W(e, ("xzmm", prev, 31))
                        elif ob < 10:
                            W(e, ("xzmm", S, 2 * (ob - 2) + 1))
                        else:
                            W(e, ("xzmm", S, 2 * ob - 3))
                        dW[ob % 2].inc(e.dma_start(
                            out=win_b[ob % 2][:],
                            in_=w_in.rearrange("(g p) o -> p g o", p=P)[:, :, ob * 128:(ob + 1) * 128]))
                    return em

                def _xzidx(k):
                    if k < 16:
                        return k // 2, k % 2, k % 2
                    return 8 + (k - 16) // 2, (k - 16) % 2, 0

                def mk_xz(k):
                    ob, th, bank = _xzidx(k)
                    def em(e, S=S, prev=prev, k=k, ob=ob, th=th, bank=bank):
                        if k == 0:
                            W(e, ("xld", S, 1))
                            W(e, ("halo", S))
                            W(e, ("xzev", prev, 31))
                        if k == 1:
                            W(e, ("xzev", prev, 15))
                            W(e, ("lnvar", prev - 2, 3))
                        W(e, ("win", S, ob))
                        if 2 <= k < 16:
                            W(e, ("xzev", S, k - 2))
                        elif k == 16:
                            W(e, ("xzev", S, 14))
                        elif k > 16:
                            W(e, ("xzev", S, k - 1))
                        for g in range(8):
                            ii = e.matmul(pp[bank][:, :], win_b[ob % 2][:, g, :],
                                          x_bf[:, g, th * 512:(th + 1) * 512],
                                          start=(g == 0), stop=(g == 7))
                        tE.inc(ii)
                    return em

                def mk_xzev(k):
                    ob, th, bank = _xzidx(k)
                    def em(e, S=S, prev=prev, k=k, ob=ob, th=th, bank=bank):
                        W(e, ("xzmm", S, k))
                        if k == 0:
                            W(e, ("convmm", prev, 15))
                        if ob < 8:
                            ii = e.copy(out=xm[:, ob, 3 + th * 512: 3 + (th + 1) * 512],
                                        in_=pp[bank][:, :])
                        else:
                            W(e, ("gate", prev, ob - 8))
                            ii = e.activation(out=silu_z[:, ob - 8, th * 512:(th + 1) * 512],
                                              in_=pp[bank][:, :], func=AF.Silu)
                        sE.inc(ii)
                    return em

                def em_hsave(e, S=S):
                    W(e, ("xzev", S, 15))
                    vE.inc(e.tensor_copy(out=halo[:], in_=xm[:, :, HQ:HQ + 3]))

                def mk_conv(c):
                    g, th = c // 2, c % 2
                    def em(e, S=S, prev=prev, c=c, g=g, th=th):
                        W(e, ("xzev", S, 2 * g + 1))
                        if c == 0:
                            W(e, ("projev", prev, 1))
                        else:
                            W(e, ("convev", S, c - 1))
                        for j in range(4):
                            ii = e.matmul(pp[2][:, :], dg_sb[:, g, j, :],
                                          xm[:, g, th * 512 + j: th * 512 + j + 512],
                                          start=(j == 0), stop=(j == 3))
                        tE.inc(ii)
                    return em

                def mk_convev(c):
                    g, th = c // 2, c % 2
                    def em(e, S=S, prev=prev, c=c, g=g, th=th):
                        W(e, ("convmm", S, c))
                        W(e, ("ic", prev, g * 16 + 15))
                        W(e, ("z", prev, g * 16 + 15))
                        ii = e.activation(out=x_conv[:, g, th * 512:(th + 1) * 512],
                                          in_=pp[2][:, :], func=AF.Silu,
                                          bias=cb_sb[:, g:g + 1], scale=1.0)
                        sE.inc(ii)
                    return em

                def mk_proj(th):
                    def em(e, S=S, prev=prev, th=th):
                        W(e, ("convev", S, 15))
                        if th == 1:
                            W(e, ("projev", S, 0))
                        for g in range(8):
                            ii = e.matmul(pp[2][0:65, :], wxp_sb[:, g, :],
                                          x_conv[:, g, th * 512:(th + 1) * 512],
                                          start=(g == 0), stop=(g == 7))
                        tE.inc(ii)
                    return em

                def mk_projev(th):
                    def em(e, S=S, prev=prev, th=th):
                        W(e, ("projmm", S, th))
                        if th == 0:
                            W(e, ("bcv", prev))
                            W(e, ("scrc", prev))
                        sl = slice(th * 512, (th + 1) * 512)
                        e.copy(out=b_bf[:, sl], in_=pp[2][0:16, :])
                        e.copy(out=c_bf[:, sl], in_=pp[2][32:48, :])
                        ii = e.activation(out=erow[:, sl], in_=pp[2][64:65, :], func=AF.Exp)
                        sE.inc(ii)
                        if th == 1:
                            ii = e.activation(out=erow[:], in_=erow[:], func=AF.Ln, bias=1.0)
                            sE.inc(ii)
                    return em

                def em_dscr(e, S=S, prev=prev, b=b, q=q):
                    W(e, ("drow", S))
                    W(e, ("da", prev, 127))
                    W(e, ("bcv", prev))
                    dD.inc(e.dma_start(out=scr_d[b, q], in_=erow[:]))
                    W(e, ("dscrw", S))
                    dD.inc(e.dma_start(
                        out=dlb[:],
                        in_=scr_d[b, q].rearrange("a t -> (a t)").partition_broadcast(P)))
                    dD.inc(e.dma_start(
                        out=bd16[:],
                        in_=scr_d[b, q].rearrange("a t -> (a t)").partition_broadcast(NST)))

                def em_bcv(e, S=S):
                    W(e, ("bd16", S))
                    vE.inc(e.tensor_tensor(out=bdel_bf[:], in0=b_bf[:], in1=bd16[:],
                                           op=ALU.mult))

                def em_scrw(e, S=S, b=b, q=q):
                    W(e, ("bcv", S))
                    dSC.inc(e.dma_start(out=scr_b[b, q], in_=bdel_bf[:]))
                    dSC.inc(e.dma_start(out=scr_c[b, q], in_=c_bf[:]))

                fr = {
                    "xld": [("G", [("X", ("xld", S, h))], mk_xld(h)) for h in range(2)],
                    "win": [("G", [(f"W{ob % 2}", ("win", S, ob))], mk_win(ob)) for ob in range(16)],
                    "halo": ("V", [("V", ("halo", S))], em_halo),
                    "xz": [("T", [("T", ("xzmm", S, k))], mk_xz(k)) for k in range(32)],
                    "xzev": [("S", [("S", ("xzev", S, k))], mk_xzev(k)) for k in range(32)],
                    "hsave": ("V", [("V", ("hsave", S))], em_hsave),
                    "conv": [("T", [("T", ("convmm", S, c))], mk_conv(c)) for c in range(16)],
                    "convev": [("S", [("S", ("convev", S, c))], mk_convev(c)) for c in range(16)],
                    "proj": [("T", [("T", ("projmm", S, th))], mk_proj(th)) for th in range(2)],
                    "projev": [
                        ("S", [("S", ("projev", S, 0))], mk_projev(0)),
                        ("S", [("S", ("projev", S, 1)), ("S", ("drow", S))], mk_projev(1))],
                    "dscr": ("Y", [("DD", ("dscrw", S)), ("DD", ("dlb", S)), ("DD", ("bd16", S))], em_dscr),
                    "bcv": ("V", [("V", ("bcv", S))], em_bcv),
                    "scrw": ("Y", [("SC", ("scrb", S)), ("SC", ("scrc", S))], em_scrw),
                }
                return fr

            def plan_wout(S, nxt):
                b, q = steps[S]
                tok0 = q * HQ
                units = {"wo": [], "xres": [], "mm": [], "res": [], "cast": [],
                         "stat": [], "fin": [], "xh": []}

                for w in range(32):
                    th, ob = w // 8, w % 8

                    def em_wo(e, S=S, w=w, th=th, ob=ob):
                        if w >= 2:
                            W(e, ("woutmm", S, w - 2))
                        dO[w % 2].inc(e.dma_start(
                            out=wo_b[w % 2][:],
                            in_=w_out.rearrange("(g p) o -> p g o", p=P)[:, :, ob * 128:(ob + 1) * 128]))
                    units["wo"].append(("G", [(f"O{w % 2}", ("wo", S, w))], em_wo))

                    def em_mm(e, S=S, nxt=nxt, w=w, th=th, ob=ob, tok0=tok0):
                        if w == 0:
                            W(e, ("gate", S, 7))
                            W(e, ("woutres", S - 1, 31))
                        else:
                            W(e, ("woutres", S, w - 1))
                        W(e, ("wo", S, w))
                        for g in range(8):
                            ii = e.matmul(pp[7][:, 0:TH], wo_b[w % 2][:, g, :],
                                          t3[S % 2][:, g, th * TH:(th + 1) * TH],
                                          start=(g == 0), stop=(g == 7))
                        tE.inc(ii)
                    units["mm"].append(("T", [("T", ("woutmm", S, w))], em_mm))

                    def em_res(e, S=S, w=w, th=th, ob=ob):
                        if ob == 0:
                            W(e, ("xres", S, th))
                        if th > 0:
                            W(e, ("lncast", S, th - 1, ob))
                        else:
                            W(e, ("lncast", S - 1, 3, ob))
                        W(e, ("woutmm", S, w))
                        ii = e.tensor_tensor(out=res[:, ob, :], in0=pp[7][:, 0:TH],
                                             in1=xres[:, ob, :], op=ALU.add)
                        vE.inc(ii)
                    units["res"].append(("V", [("V", ("woutres", S, w))], em_res))

                    def em_cast(e, S=S, w=w, th=th, ob=ob):
                        W(e, ("woutres", S, w))
                        if ob >= 2:
                            W(e, ("lnstat", S, th, ob - 2))
                        elif th > 0:
                            W(e, ("lnstat", S, th - 1, 6 + ob))
                        else:
                            W(e, ("lnstat", S - 1, 3, 6 + ob))
                        e.copy(out=rsbf[:, ob % 2, 0:TH], in_=res[:, ob, :])
                        ii = e.activation(out=rsbf[:, ob % 2, TH:2 * TH], in_=res[:, ob, :],
                                          func=AF.Square)
                        sE.inc(ii)
                    units["cast"].append(("S", [("S", ("lncast", S, th, ob))], em_cast))

                    def em_stat(e, S=S, w=w, th=th, ob=ob):
                        if ob == 0:
                            if th == 0:
                                W(e, ("xzev", S + 2, 15))
                                W(e, ("lnvar", S - 1, 3))
                            else:
                                W(e, ("lnm", S, th - 1))
                                W(e, ("lnvar", S, th - 1))
                        W(e, ("lncast", S, th, ob))
                        ii = e.matmul(pp[1][0:1, 0:512], ones_sb[:], rsbf[:, ob % 2, :],
                                      start=(ob == 0), stop=(ob == 7),
                                      skip_group_check=True)
                        tE.inc(ii)
                    units["stat"].append(("T", [("T", ("lnstat", S, th, ob))], em_stat))

                for th in range(4):
                    def em_xres(e, S=S, b=b, th=th, tok0=tok0):
                        if th == 0:
                            W(e, ("woutres", S - 1, 31))
                        else:
                            W(e, ("woutres", S, 8 * (th - 1) + 7))
                        dR.inc(e.dma_start(
                            out=xres[:],
                            in_=x_fm[b].rearrange("(g p) t -> p g t", p=P)[
                                :, :, tok0 + th * TH:tok0 + (th + 1) * TH]))
                    units["xres"].append(("Y", [("R", ("xres", S, th))], em_xres))

                    def em_lnm(e, S=S, th=th):
                        W(e, ("lnstat", S, th, 7))
                        W(e, ("rowsbc", S, th - 1) if th > 0 else ("rowsbc", S - 1, 3))
                        e.copy(out=mrow[:], in_=pp[1][0:1, 0:256])
                        ii = e.activation(out=srow[:], in_=pp[1][0:1, 0:256], func=AF.Square)
                        sE.inc(ii)
                    def em_lnvar(e, S=S, th=th):
                        W(e, ("lnm", S, th))
                        ii = e.tensor_tensor(out=srow[:], in0=pp[1][0:1, 256:512],
                                             in1=srow[:], op=ALU.subtract)
                        vE.inc(ii)
                    def em_lnrstd(e, S=S, th=th):
                        W(e, ("lnvar", S, th))
                        e.activation(out=srow[:], in_=srow[:], func=AF.Ln,
                                     bias=eps_sb[0:1, 0:1])
                        ii = e.activation(out=srow[:], in_=srow[:], func=AF.Exp,
                                          scale=negh_sb[0:1, 0:1])
                        sE.inc(ii)
                    def em_lnmr(e, S=S, th=th):
                        W(e, ("lnrstd", S, th))
                        ii = e.tensor_tensor(out=mrow[:], in0=mrow[:],
                                             in1=srow[:], op=ALU.mult)
                        vE.inc(ii)
                    def em_rows(e, S=S, th=th):
                        W(e, ("lnmr", S, th))
                        dRW.inc(e.dma_start(out=row_scr[0:1, 0:TH], in_=mrow[:]))
                        dRW.inc(e.dma_start(out=row_scr[0:1, TH:2 * TH], in_=srow[:]))
                        W(e, ("roww", S, th))
                        dRW.inc(e.dma_start(out=mrb[:],
                                    in_=row_scr[0:1, 0:TH].rearrange("a t -> (a t)")
                                    .partition_broadcast(P)))
                        dRW.inc(e.dma_start(out=rsb[:],
                                    in_=row_scr[0:1, TH:2 * TH].rearrange("a t -> (a t)")
                                    .partition_broadcast(P)))
                    def em_xh(e, S=S, th=th):
                        W(e, ("rowsbc", S, th))
                        if th == 0:
                            W(e, ("xhstore", S - 1, 3))
                        else:
                            W(e, ("xhstore", S, th - 1))
                        for ob in range(8):
                            e.tensor_tensor(out=res[:, ob, :], in0=res[:, ob, :], in1=mrb[:],
                                            op=ALU.subtract)
                            ii = e.tensor_tensor(out=xho[:, ob, :], in0=res[:, ob, :],
                                                 in1=rsb[:], op=ALU.mult)
                            vE.inc(ii)
                    def em_xhst(e, S=S, b=b, th=th, tok0=tok0):
                        W(e, ("lnxh", S, th))
                        dH.inc(e.dma_start(
                            out=xhat_scr[b].rearrange("(g p) t -> p g t", p=P)[
                                :, :, tok0 + th * TH:tok0 + (th + 1) * TH],
                            in_=xho[:]))
                    units["fin"].append([
                        ("S", [("S", ("lnm", S, th))], em_lnm),
                        ("V", [("V", ("lnvar", S, th))], em_lnvar),
                        ("S", [("S", ("lnrstd", S, th))], em_lnrstd),
                        ("V", [("V", ("lnmr", S, th))], em_lnmr),
                        ("Y", [("RW", None), ("RW", ("roww", S, th)), ("RW", None),
                               ("RW", ("rowsbc", S, th))], em_rows),
                    ])
                    units["xh"].append([
                        ("V", [("V", None)] * 7 + [("V", ("lnxh", S, th))], em_xh),
                        ("Y", [("H", ("xhstore", S, th))], em_xhst),
                    ])
                return units

            def plan_scan(S):
                b, q = steps[S]

                def mk_bb(blk):
                    nbb = blk % 8
                    def em(e, S=S, blk=blk, nbb=nbb):
                        if blk < 2:
                            W(e, ("scrb", S))
                            W(e, ("z", S - 1, 2 * (blk + 62) + 1))
                        else:
                            W(e, ("z", S, 2 * (blk - 2) + 1))
                        dB[nbb % 2].inc(e.dma_start(
                            out=bbv[nbb % 2],
                            in_=scr_b[b, q, nbb * 2:(nbb + 1) * 2, :]
                            .rearrange("n t -> (n t)").partition_broadcast(P)))
                    return em

                def mk_cb(blk):
                    nbb = blk % 8
                    def em(e, S=S, blk=blk, nbb=nbb):
                        if blk < 2:
                            W(e, ("scrc", S))
                            W(e, ("z", S - 1, 2 * (blk + 62) + 1))
                        else:
                            W(e, ("z", S, 2 * (blk - 2) + 1))
                        dC[nbb % 2].inc(e.dma_start(
                            out=cbv[nbb % 2],
                            in_=scr_c[b, q, nbb * 2:(nbb + 1) * 2, :]
                            .rearrange("n t -> (n t)").partition_broadcast(P)))
                    return em

                def mk_da(blk):
                    g, nbb = blk // 8, blk % 8
                    def em(e, S=S, blk=blk, g=g, nbb=nbb):
                        for j in range(2):
                            i = blk * 2 + j
                            col = g * 16 + nbb * 2 + j
                            if i >= 2:
                                W(e, ("z", S, i - 2))
                            if i == 0:
                                W(e, ("dlb", S))
                            ii = e.activation(out=da_v[i % 2], in_=dlb[:],
                                              func=AF.Exp, scale=a_sb[:, col:col + 1])
                            sE.inc(ii)
                    return em

                def mk_tiles(blk):
                    g, nbb = blk // 8, blk % 8
                    slot = nbb % 2
                    def em(e, S=S, blk=blk, g=g, nbb=nbb, slot=slot, q=q):
                        for j in range(2):
                            i = blk * 2 + j
                            col = g * 16 + nbb * 2 + j
                            k = i % 2
                            if i >= 4:
                                W(e, ("ic", S, i - 4))
                            if j == 0:
                                W(e, ("bb", S, blk))
                                W(e, ("cb", S, blk))
                            if q == 0 and i >= 2:
                                e.tensor_copy(out=hcar[:, i - 2:i - 1],
                                              in_=h_b[k][:, HQ - 1:HQ])
                            e.tensor_tensor(out=db_b[k][:], in0=x_conv[:, g, :],
                                            in1=bbv[slot][:, j, :], op=ALU.mult)
                            W(e, ("da", S, i))
                            init = 0.0 if q == 0 else hcar[:, col:col + 1]
                            ii = e.tensor_tensor_scan(
                                out=h_b[k][:], data0=da_v[i % 2], data1=db_b[k][:],
                                initial=init, op0=ALU.mult, op1=ALU.add)
                            ii = e.tensor_tensor(out=z_b[i % 4][:], in0=h_b[k][:],
                                                 in1=cbv[slot][:, j, :], op=ALU.mult)
                            vE.inc(ii)
                    incs = []
                    for j in range(2):
                        i = blk * 2 + j
                        incs.append(("V", ("z", S, i)))
                    return incs, em

                def mk_gate(g):
                    yp = (pp[3], pp[4]) if g % 2 == 0 else (pp[5], pp[6])
                    def em(e, S=S, g=g, yp=yp, q=q):
                        if g == 7 and q == 0:
                            e.tensor_copy(out=hcar[:, 126:127],
                                          in_=h_b[0][:, HQ - 1:HQ])
                            e.tensor_copy(out=hcar[:, 127:128],
                                          in_=h_b[1][:, HQ - 1:HQ])
                        W(e, ("ic", S, g * 16 + 15))
                        W(e, ("xzev", S, 17 + 2 * g))
                        W(e, ("woutmm", S - 2, 31))
                        e.tensor_tensor(out=t3[S % 2][:, g, 0:512], in0=yp[0][:, :],
                                        in1=silu_z[:, g, 0:512], op=ALU.mult)
                        ii = e.tensor_tensor(out=t3[S % 2][:, g, 512:1024], in0=yp[1][:, :],
                                             in1=silu_z[:, g, 512:1024], op=ALU.mult)
                        vE.inc(ii)
                    return em

                def mk_tid(blk):
                    g, nbb = blk // 8, blk % 8
                    yp = (pp[3], pp[4]) if g % 2 == 0 else (pp[5], pp[6])
                    def em(e, S=S, blk=blk, g=g, nbb=nbb, yp=yp):
                        if nbb == 0:
                            if g >= 2:
                                W(e, ("gate", S, g - 2))
                            else:
                                W(e, ("gate", S - 1, 6 + g))
                        for j in range(2):
                            i = blk * 2 + j
                            W(e, ("z", S, i))
                            first = (j == 0 and nbb == 0)
                            e.matmul(yp[0][:, :], i128_sb[:], z_b[i % 4][:, 0:512],
                                     start=first, stop=False)
                            ii = e.matmul(yp[1][:, :], i128_sb[:], z_b[i % 4][:, 512:1024],
                                          start=first, stop=False)
                            if j == 1 and nbb == 7:
                                e.matmul(yp[0][:, :], dd_sb[:, g, :],
                                         x_conv[:, g, 0:512], start=False, stop=True)
                                ii = e.matmul(yp[1][:, :], dd_sb[:, g, :],
                                              x_conv[:, g, 512:1024],
                                              start=False, stop=True)
                            tE.inc(ii)
                    return em

                return dict(mk_bb=mk_bb, mk_cb=mk_cb, mk_da=mk_da, mk_tiles=mk_tiles,
                            mk_gate=mk_gate, mk_tid=mk_tid)

            NSTEP = len(steps)
            fronts = {0: plan_front(0, prev=-1)}
            f0 = fronts[0]
            for u in f0["xld"]:
                op(*u)
            op(*f0["halo"])
            for ob in range(8):
                op(*f0["win"][ob])
            for k in range(16):
                op(*f0["xz"][k]); op(*f0["xzev"][k])
            op(*f0["hsave"])
            for ob in range(8, 16):
                op(*f0["win"][ob])
            for k in range(16, 32):
                op(*f0["xz"][k]); op(*f0["xzev"][k])
            for c in range(16):
                op(*f0["conv"][c]); op(*f0["convev"][c])
            for t_ in range(2):
                op(*f0["proj"][t_]); op(*f0["projev"][t_])
            op(*f0["dscr"]); op(*f0["bcv"]); op(*f0["scrw"])

            wout_units = {}
            for S in range(NSTEP):
                sc = plan_scan(S)
                if S + 1 < NSTEP:
                    fr = plan_front(S + 1, prev=S)
                    fronts[S + 1] = fr
                else:
                    fr = None
                wo = wout_units.get(S - 1)

                PLACE = {}
                def at(B, unit):
                    PLACE.setdefault(B, []).append(unit)

                at(13, ("V", [("V", ("gate", S, 0))], sc["mk_gate"](0)))
                for g in range(1, 7):
                    at(8 * g + 9, ("V", [("V", ("gate", S, g))], sc["mk_gate"](g)))

                if wo is not None:
                    for th in range(4):
                        at(15 if th == 0 else 13 + 12 * th, wo["xres"][th])
                        for ob in range(8):
                            w = 8 * th + ob
                            B0 = 16 + 12 * th + ob
                            at(B0 - 1, wo["wo"][w])
                            at(B0, wo["mm"][w])
                            at(B0 + 1, wo["res"][w])
                            at(B0 + 2, wo["cast"][w])
                            at(B0 + 3, wo["stat"][w])
                        fin = wo["fin"][th]
                        at(27 + 12 * th, fin[0]); at(27 + 12 * th, fin[1])
                        at(28 + 12 * th, fin[2]); at(28 + 12 * th, fin[3])
                        at(28 + 12 * th, fin[4])
                        at(29 + 12 * th, wo["xh"][th][0])
                        at(30 + 12 * th, wo["xh"][th][1])

                if fr is not None:
                    at(0, fr["halo"])
                    at(0, fr["win"][0]); at(0, fr["win"][1])
                    at(1, fr["xld"][0]); at(3, fr["xld"][1])
                    for ob in range(2, 8):
                        at(2 * ob + 2, fr["win"][ob])
                    for k in range(16):
                        at(3 + k, fr["xz"][k])
                        at(4 + k, fr["xzev"][k])
                    at(20, fr["hsave"])
                    for ob in range(8, 16):
                        at(20 + 6 * (ob - 8), fr["win"][ob])
                    for k in range(16, 32):
                        gp, th2 = (k - 16) // 2, (k - 16) % 2
                        at(21 + 6 * gp + 2 * th2, fr["xz"][k])
                        at(22 + 6 * gp + 2 * th2, fr["xzev"][k])
                    lastev = 20
                    for c in range(16):
                        bcv_ = max(21 + 2 * c, lastev + 1)
                        bev_ = max(bcv_ + 1, 8 * (c // 2) + 8)
                        at(bcv_, fr["conv"][c])
                        at(bev_, fr["convev"][c])
                        lastev = bev_

                for blk in range(64):
                    op("G", [(f"B{blk % 2}", ("bb", S, blk))], sc["mk_bb"](blk))
                    op("Y", [(f"C{blk % 2}", ("cb", S, blk))], sc["mk_cb"](blk))
                    op("S", [("S", ("da", S, blk * 2 + j)) for j in range(2)],
                       sc["mk_da"](blk))
                    incs, em = sc["mk_tiles"](blk)
                    op("V", incs, em)
                    op("T", [("T", ("ic", S, blk * 2 + j)) for j in range(2)],
                       sc["mk_tid"](blk))
                    for u in PLACE.get(blk, []):
                        op(*u)
                op("V", [("V", ("gate", S, 7))], sc["mk_gate"](7))
                for B in sorted(k2 for k2 in PLACE if k2 >= 64):
                    for u in PLACE[B]:
                        op(*u)
                if fr is not None:
                    for t_ in range(2):
                        op(*fr["proj"][t_]); op(*fr["projev"][t_])
                    op(*fr["dscr"]); op(*fr["bcv"]); op(*fr["scrw"])
                wout_units[S] = plan_wout(S, nxt=S + 1)

            wo = wout_units[NSTEP - 1]
            for th in range(4):
                op(*wo["xres"][th])
                for ob in range(8):
                    w = th * 8 + ob
                    op(*wo["wo"][w]); op(*wo["mm"][w])
                    op(*wo["res"][w]); op(*wo["cast"][w]); op(*wo["stat"][w])
                for u in wo["fin"][th]:
                    op(*u)
                for u in wo["xh"][th]:
                    op(*u)

            sim = {k: c.n for k, c in CTRS.items()}
            for eng, incs, _ in PLAN:
                for cname, label in incs:
                    sim[cname] += CTRS[cname].step
                    if label is not None:
                        C[label] = (cname, sim[cname])

            for eng_name, sect in (("G", block.gpsimd), ("Y", block.sync),
                                   ("S", block.scalar), ("V", block.vector),
                                   ("T", block.tensor)):
                fns = [em for e2, _, em in PLAN if e2 == eng_name]
                if fns:
                    sect(lambda e, fns=fns: [f(e) for f in fns])

            for k, c in CTRS.items():
                assert c.n == sim[k], (k, c.n, sim[k])

            barrier()

        with contextlib.ExitStack() as s2s:
            e2 = s2s.enter_context
            w1_sb = e2(nc.sbuf_tensor([P, 8, DF], F8E5))
            w2_sb = e2(nc.sbuf_tensor([P, 32, D], F8E5))
            xh_b = [e2(nc.sbuf_tensor(f"xh{i}", [P, 8, S2], F32)) for i in range(2)]
            xh8 = e2(nc.sbuf_tensor([P, 8, S2], F8E5))
            hf8 = e2(nc.sbuf_tensor([P, 32, S2], F8E5))
            msk_b = [e2(nc.sbuf_tensor(f"msk{i}", [P, 8, S2], BF16)) for i in range(2)]
            dr2 = e2(nc.sbuf_tensor([P, 8, S2], F32))
            rs2 = e2(nc.sbuf_tensor([P, 8, S2], F32))
            rbf2 = e2(nc.sbuf_tensor([P, 2, S2], BF16))
            sbf2 = e2(nc.sbuf_tensor([P, 2, S2], BF16))
            mrow2 = e2(nc.sbuf_tensor([1, 2 * S2], F32))
            srow2 = e2(nc.sbuf_tensor([1, S2], F32))
            mrb2 = e2(nc.sbuf_tensor([P, S2], F32))
            rsb2 = e2(nc.sbuf_tensor([P, S2], F32))

            def g2_(g):
                dG.inc(g.dma_start(out=w1_sb[:],
                                   in_=w1t.rearrange("(g p) n -> p g n", p=P)))
                dG.inc(g.dma_start(out=w2_sb[:],
                                   in_=w2t.rearrange("(k p) o -> p k o", p=P)))
            block.gpsimd(g2_)
            w_d = dG.n

            prev_vh1 = None
            prev_f2 = None

            slices = [(b, si) for b in range(NB) for si in range(T // S2)]
            for gi, (b, si) in enumerate(slices):
                tok = si * S2
                bi = gi % 2
                D0, G0, S0, V0, T0 = dS.n, dG.n, sE.n, vE.n, tE.n
                VX = V0 + 1
                VD, VV2, VR2 = V0 + 2, V0 + 3, V0 + 4
                VO2 = V0 + 12

                def gl_(gq, b=b, tok=tok, bi=bi, gi=gi):
                    if gi >= 2:
                        gq.wait_ge(vE.sem, vd_hist[gi - 2])
                    dG.inc(gq.dma_start(
                        out=xh_b[bi][:],
                        in_=xhat_scr[b].rearrange("(g p) t -> p g t", p=P)[:, :, tok:tok + S2]))
                    dG.inc(gq.dma_start(
                        out=msk_b[bi][:],
                        in_=mask_fm[b].rearrange("(g p) t -> p g t", p=P)[:, :, tok:tok + S2]))
                block.gpsimd(gl_)

                def y_(y, b=b, tok=tok, bi=bi, gi=gi):
                    y.wait_ge(vE.sem, VR2)
                    dS.inc(y.dma_start(out=row_scr2[0:1, 0:S2], in_=mrow2[:, 0:S2]))
                    dS.inc(y.dma_start(out=row_scr2[0:1, S2:2 * S2], in_=srow2[:]))
                    y.wait_ge(dS.sem, dS.n)
                    dS.inc(y.dma_start(out=mrb2[:],
                                in_=row_scr2[0:1, 0:S2].rearrange("a t -> (a t)")
                                .partition_broadcast(P)))
                    dS.inc(y.dma_start(out=rsb2[:],
                                in_=row_scr2[0:1, S2:2 * S2].rearrange("a t -> (a t)")
                                .partition_broadcast(P)))
                    y.wait_ge(vE.sem, VO2)
                    dS.inc(y.dma_start(
                        out=out_fm[b].rearrange("(g p) t -> p g t", p=P)[:, :, tok:tok + S2],
                        in_=dr2[:]))
                block.sync(y_)
                ld_d = G0 + 32
                rows_bc = D0 + 64
                vd_hist = vd_hist if gi else {}

                def tt_(t, gi=gi, S0=S0, V0=V0):
                    t.wait_ge(vE.sem, VX)
                    if gi == 0:
                        t.wait_ge(dG.sem, w_d)
                    H1B = (0, 1, 6, 7)
                    for nb2 in range(32):
                        if nb2 >= 4:
                            t.wait_ge(sE.sem, S0 + nb2 - 3)
                        for gg in range(4):
                            ii = t.matmul(
                                pp[H1B[nb2 % 4]][:, 0:S2],
                                w1_sb[:, 2 * gg:2 * gg + 2, nb2 * 128:(nb2 + 1) * 128],
                                xh8[:, 2 * gg:2 * gg + 2, :],
                                start=(gg == 0), stop=(gg == 3),
                                perf_mode=mybir.MatmulPerfMode.DoubleRow)
                        tE.inc(ii)
                    t.wait_ge(sE.sem, S0 + 32)
                    for ob in range(8):
                        if ob >= 2:
                            t.wait_ge(sE.sem, S0 + 32 + ob - 1)
                        for kk in range(16):
                            ii = t.matmul(
                                pp[2 + ob % 2][:, 0:S2],
                                w2_sb[:, 2 * kk:2 * kk + 2, ob * 128:(ob + 1) * 128],
                                hf8[:, 2 * kk:2 * kk + 2, :],
                                start=(kk == 0), stop=(kk == 15),
                                perf_mode=mybir.MatmulPerfMode.DoubleRow)
                        tE.inc(ii)
                    for ob in range(8):
                        t.wait_ge(sE.sem, S0 + 40 + ob + 1)
                        t.matmul(pp[4][0:1, 0:S2], ones_sb[:], rbf2[:, ob % 2, :],
                                 start=(ob == 0), stop=(ob == 7), skip_group_check=True)
                        ii = t.matmul(pp[5][0:1, 0:S2], ones_sb[:], sbf2[:, ob % 2, :],
                                      start=(ob == 0), stop=(ob == 7), skip_group_check=True)
                        tE.inc(ii)
                block.tensor(tt_)

                def ss_(s, T0=T0, V0=V0, D0=D0, prev_f2=prev_f2):
                    if prev_f2 is not None:
                        s.wait_ge(tE.sem, prev_f2)
                    for nb2 in range(32):
                        s.wait_ge(tE.sem, T0 + nb2 + 1)
                        ii = s.activation(out=hf8[:, nb2, :],
                                          in_=pp[(0, 1, 6, 7)[nb2 % 4]][:, 0:S2],
                                          func=AF.Relu, bias=b1q_sb[:, nb2:nb2 + 1],
                                          scale=1.0)
                        sE.inc(ii)
                    for ob in range(8):
                        s.wait_ge(tE.sem, T0 + 32 + ob + 1)
                        ii = s.activation(out=dr2[:, ob, :], in_=pp[2 + ob % 2][:, 0:S2],
                                          func=AF.Identity, bias=b2q_sb[:, ob:ob + 1],
                                          scale=1.0)
                        sE.inc(ii)
                    s.wait_ge(vE.sem, VD)
                    for ob in range(8):
                        if ob >= 2:
                            s.wait_ge(tE.sem, T0 + 40 + ob - 1)
                        s.copy(out=rbf2[:, ob % 2, :], in_=rs2[:, ob, :])
                        ii = s.activation(out=sbf2[:, ob % 2, :], in_=rs2[:, ob, :],
                                          func=AF.Square)
                        sE.inc(ii)
                    s.wait_ge(tE.sem, T0 + 48)
                    s.wait_ge(dS.sem, D0)
                    s.copy(out=mrow2[:, 0:S2], in_=pp[4][0:1, 0:S2])
                    ii = s.activation(out=srow2[:], in_=pp[4][0:1, 0:S2], func=AF.Square)
                    sE.inc(ii)
                    s.wait_ge(vE.sem, VV2)
                    s.activation(out=mrow2[:, S2:2 * S2], in_=srow2[:], func=AF.Ln,
                                 bias=eps_sb[0:1, 0:1])
                    ii = s.activation(out=srow2[:], in_=mrow2[:, S2:2 * S2], func=AF.Exp,
                                      scale=negh_sb[0:1, 0:1])
                    sE.inc(ii)
                block.scalar(ss_)

                def vv_(v, bi=bi, gi=gi, S0=S0, T0=T0, D0=D0, ld_d=ld_d,
                        rows_bc=rows_bc, prev_f2=prev_f2):
                    v.wait_ge(dG.sem, ld_d)
                    v.wait_ge(dS.sem, D0)
                    if gi == 0:
                        v.wait_ge(dG.sem, w_d)
                    if prev_f2 is not None:
                        v.wait_ge(tE.sem, prev_f2)
                    ii = v.tensor_copy(
                        out=xh8[:].rearrange("p a t -> p (a t)"),
                        in_=xh_b[bi][:].rearrange("p a t -> p (a t)"))
                    vE.inc(ii)
                    v.wait_ge(sE.sem, S0 + 40)
                    v.tensor_tensor(out=dr2[:].rearrange("p a t -> p (a t)"),
                                    in0=dr2[:].rearrange("p a t -> p (a t)"),
                                    in1=msk_b[bi][:].rearrange("p a t -> p (a t)"),
                                    op=ALU.mult)
                    ii = v.tensor_tensor(out=rs2[:].rearrange("p a t -> p (a t)"),
                                         in0=dr2[:].rearrange("p a t -> p (a t)"),
                                         in1=xh_b[bi][:].rearrange("p a t -> p (a t)"),
                                         op=ALU.add)
                    vE.inc(ii)
                    v.wait_ge(sE.sem, S0 + 49)
                    ii = v.tensor_tensor(out=srow2[:], in0=pp[5][0:1, 0:S2], in1=srow2[:],
                                         op=ALU.subtract)
                    vE.inc(ii)
                    v.wait_ge(sE.sem, S0 + 50)
                    ii = v.tensor_tensor(out=mrow2[:, 0:S2], in0=mrow2[:, 0:S2],
                                         in1=srow2[:], op=ALU.mult)
                    vE.inc(ii)
                    v.wait_ge(dS.sem, rows_bc)
                    for ob in range(8):
                        v.tensor_tensor(out=rs2[:, ob, :], in0=rs2[:, ob, :], in1=mrb2[:],
                                        op=ALU.subtract)
                        ii = v.tensor_tensor(out=dr2[:, ob, :], in0=rs2[:, ob, :],
                                             in1=rsb2[:], op=ALU.mult)
                        vE.inc(ii)
                block.vector(vv_)
                vd_hist[gi] = VD
                prev_f2 = T0 + 40
                assert vE.n == VO2 and sE.n == S0 + 50 and tE.n == T0 + 48
                assert dS.n == D0 + 80 and dG.n == G0 + 32

            def yfin_(y):
                y.wait_ge(dS.sem, dS.n)
                y.wait_ge(dG.sem, dG.n)
            block.sync(yfin_)

    return nc


_NC_CACHE = {}


def kernel(L0_attn_x, L0_attn_W_in, L0_attn_conv_w, L0_attn_conv_b,
           L0_attn_W_xproj, L0_attn_log_A, L0_attn_D_param, L0_attn_W_out,
           L0_ffn_W1, L0_ffn_b1, L0_ffn_W2, L0_ffn_b2):
    x = np.asarray(L0_attn_x, np.float32)
    w_in = np.ascontiguousarray(np.asarray(L0_attn_W_in, np.float32))
    conv_w = np.asarray(L0_attn_conv_w, np.float32)
    conv_b = np.asarray(L0_attn_conv_b, np.float32)
    w_xproj = np.asarray(L0_attn_W_xproj, np.float32)
    log_a = np.asarray(L0_attn_log_A, np.float32)
    d_param = np.asarray(L0_attn_D_param, np.float32)
    w_out = np.ascontiguousarray(np.asarray(L0_attn_W_out, np.float32))
    w1 = np.asarray(L0_ffn_W1, np.float32)
    b1 = np.asarray(L0_ffn_b1, np.float32)
    w2 = np.asarray(L0_ffn_W2, np.float32)
    b2 = np.asarray(L0_ffn_b2, np.float32)

    wxp_pad = np.zeros((D, 65), np.float32)
    wxp_pad[:, 0:16] = w_xproj[:, 0:16]
    wxp_pad[:, 32:48] = w_xproj[:, 16:32]
    wxp_pad[:, 64] = w_xproj[:, 32]
    w1t = np.ascontiguousarray(w1.T)
    w2t = np.ascontiguousarray(w2.T)
    loga_rr = np.ascontiguousarray(
        log_a.reshape(8, P, NST).transpose(1, 0, 2).reshape(P, 128))
    dgconv = np.zeros((8, 4, P, P), np.float32)
    for g in range(8):
        for j in range(4):
            np.fill_diagonal(dgconv[g, j], conv_w[g * P:(g + 1) * P, 0, j])
    dgd = np.zeros((8, P, P), np.float32)
    for g in range(8):
        np.fill_diagonal(dgd[g], d_param[g * P:(g + 1) * P])
    cbcol = np.ascontiguousarray(conv_b.reshape(8, P).T)
    b1rr = np.ascontiguousarray(b1.reshape(32, P).T)
    b2rr = np.ascontiguousarray(b2.reshape(8, P).T)
    i128a = np.eye(P, dtype=ml_dtypes.bfloat16)
    onesa = np.full((P, 1), 1.0 / 1024.0, dtype=ml_dtypes.bfloat16)

    import jax
    with jax.default_device(jax.devices("cpu")[0]):
        try:
            key42 = jax.random.key(42, impl="threefry2x32")
        except TypeError:
            key42 = jax.random.key(42)
        keep = np.asarray(jax.random.bernoulli(key42, 0.8, (16, T, D)))
    maskv = np.where(keep, np.float32(1.25), np.float32(0.0)).astype(ml_dtypes.bfloat16)

    if "nc" not in _NC_CACHE:
        _NC_CACHE["nc"] = build_program()
    nc = _NC_CACHE["nc"]

    in_maps = []
    for c in range(NCORE):
        xs = np.ascontiguousarray(x[2 * c:2 * c + 2].transpose(0, 2, 1))
        ms = np.ascontiguousarray(maskv[2 * c:2 * c + 2].transpose(0, 2, 1))
        in_maps.append(dict(
            x_fm=xs, w_in=w_in, w_out=w_out, wxp=wxp_pad, w1t=w1t, w2t=w2t,
            loga_r=loga_rr, dgconv=dgconv, dgd=dgd, cbcol=cbcol, b1r=b1rr,
            b2r=b2rr, mask_fm=ms, i128=i128a, onesln=onesa,
        ))

    res = run_bass_kernel_spmd(nc, in_maps, core_ids=list(range(NCORE)))
    outs = [np.asarray(res.results[c]["out_fm"]).transpose(0, 2, 1)
            for c in range(NCORE)]
    return np.ascontiguousarray(np.concatenate(outs, axis=0))
